# version 03
# baseline (speedup 1.0000x reference)
import numpy as np
import ml_dtypes
import concourse.bass as bass
import concourse.mybir as mybir
from concourse.bass_utils import run_bass_kernel_spmd
from contextlib import ExitStack

F32 = mybir.dt.float32
BF16 = mybir.dt.bfloat16
F16 = mybir.dt.float16
AF = mybir.ActivationFunctionType
ALU = mybir.AluOpType
AX = mybir.AxisListType

NC_ = 8
P = 128
T = 1024
NTT = 8
D = 2048
H = 16
QR, KVR, ROPE, NOPE, VD = 768, 512, 64, 128, 128
QKD = NOPE + ROPE
SCALE = float(QKD) ** -0.5
MAGIC = 1536.0
STAGE = 4

KHI_OFF = 0
KLO_OFF = KHI_OFF + H * P * T
V_OFF = KLO_OFF + ROPE * T
KN_OFF = V_OFF + H * T * VD
SZKV = KN_OFF + 128

ENGS = ("sync", "gpsimd", "tensor", "vector", "scalar")
Z = ("vS", 0)


def A(t, off, *dims):
    return bass.AP(t, off, [list(d) for d in dims])


class Plan:
    def __init__(self, nc, es):
        self.nc = nc
        self.tapes = {e: [] for e in ENGS}
        self.sems = {}
        self.last_wait = {}
        for name in ("dS", "dG", "cc", "tS", "vS", "sS",
                     "bSY", "bGP", "bT", "bV", "bS"):
            h = es.enter_context(nc.semaphore(name))
            self.sems[name] = [h, 0]

    def val(self, sem):
        return (sem, self.sems[sem][1])

    def run(self, eng, fn):
        self.tapes[eng].append(fn)

    def inc(self, eng, fn, sem, k=1):
        rec = self.sems[sem]
        rec[1] += k
        h, v = rec[0], rec[1]
        self.tapes[eng].append(lambda e, fn=fn, h=h, k=k: fn(e).then_inc(h, k))
        return (sem, v)

    def wait(self, eng, *rels):
        best = {}
        for rel in rels:
            if rel is None:
                continue
            s, v = rel
            if v > best.get(s, 0):
                best[s] = v
        for s, v in best.items():
            key = (eng, s)
            if self.last_wait.get(key, -1) >= v:
                continue
            self.last_wait[key] = v
            h = self.sems[s][0]
            self.tapes[eng].append(lambda e, h=h, v=v: e.wait_ge(h, v))

    def dma(self, eng, out_ap, in_ap, sem):
        return self.inc(eng, lambda e, o=out_ap, i=in_ap:
                        e.dma_start(out=o, in_=i), sem, 16)

    def barrier(self):
        e2s = {"sync": "bSY", "gpsimd": "bGP", "tensor": "bT",
               "vector": "bV", "scalar": "bS"}
        self.wait("sync", self.val("dS"))
        self.wait("gpsimd", self.val("dG"), self.val("cc"))
        for e, s in e2s.items():
            rec = self.sems[s]
            rec[1] += 1
            h = rec[0]
            self.tapes[e].append(lambda x, h=h: x.sem_inc(h, 1))
        for e in ENGS:
            self.wait(e, *[self.val(s) for e2, s in e2s.items() if e2 != e])


class MK:
    def __init__(self, stage=STAGE):
        self.stage = stage
        nc = bass.Bass("TRN2", target_bir_lowering=False,
                       detect_race_conditions=False)
        self.nc = nc
        self.es = ExitStack()
        self.p = Plan(nc, self.es)
        dr = nc.dram_tensor
        self.x0_d = dr("x0", [T, D], F32, kind="ExternalInput")
        self.wqa_d = dr("wqa", [D, QR], F32, kind="ExternalInput")
        self.gqa_d = dr("gqa", [P, QR // P], F32, kind="ExternalInput")
        self.wqb_d = dr("wqb", [QR, H * QKD], F32, kind="ExternalInput")
        self.wkva_d = dr("wkva", [D, KVR + ROPE], F32, kind="ExternalInput")
        self.gkva_d = dr("gkva", [P, KVR // P], F32, kind="ExternalInput")
        self.wkvb_d = dr("wkvb", [KVR, H * (NOPE + VD)], F32, kind="ExternalInput")
        self.wo_d = dr("wo", [H * VD, D], F32, kind="ExternalInput")
        self.wffn_d = dr("wffn", [D, D], F32, kind="ExternalInput")
        self.brep_d = dr("brep", [P, D], F32, kind="ExternalInput")
        self.cos_d = dr("cosw", [T, ROPE], F32, kind="ExternalInput")
        self.sn_d = dr("snw", [T, ROPE], F32, kind="ExternalInput")
        self.qflag_d = dr("qflag", [NTT, T], F32, kind="ExternalInput")
        self.krows_d = dr("krows", [NTT, NC_ * T], F32, kind="ExternalInput")
        self.trim_d = dr("trim", [P, P], F32, kind="ExternalInput")
        self.iden_d = dr("iden", [P, P], F32, kind="ExternalInput")
        self.out_d = dr("out", [T, D], F32, kind="ExternalOutput")
        self.kvown = dr("kvown", [SZKV], BF16)
        self.kvall = dr("kvall", [NC_ * SZKV], BF16)
        self.weffT_d = dr("weffT", [D, D], BF16)
        self.wqaB = dr("wqaB", [D, QR], BF16)
        self.wqbB = dr("wqbB", [QR, H * QKD], BF16)
        self.wkvaB = dr("wkvaB", [D, KVR + ROPE], BF16)
        self.wkvbB = dr("wkvbB", [KVR, H * (NOPE + VD)], BF16)
        self.woB = dr("woB", [H * VD, D], BF16)
        self.wsrowD = dr("wsrowD", [1, D], F32)
        self.qhi_dd = dr("qhid", [H * NOPE * T], BF16)
        self.qlo_dd = dr("qlod", [H * 65 * T], BF16)

        L = lambda nm, shp, dt: self.es.enter_context(
            nc.sbuf_tensor(nm, shp, dt, side="left"))
        self.xres = L("xres", [P, NTT * D], F32)
        self.klo_all = L("klo_all", [73, NC_ * T], BF16)
        self.trim_b = L("trim_b", [P, P], BF16)
        self.iden_b = L("iden_b", [P, P], BF16)
        self.iden_f = L("iden_f", [P, P], F32)
        self.ones_c = L("ones_c", [P, 1], BF16)
        self.ones_r = L("ones_r", [1, P], BF16)
        self.stats = L("stats", [P, 64], F32)
        self.xs_all = L("xs_all", [P, NTT], F32)
        self.kmaxbc = L("kmaxbc", [P, H], F32)
        self.knrow_b = L("knrow_b", [1, P], BF16)
        self.gqs_c = L("gqs_c", [P, QR // P], F32)
        self.gkv_c = L("gkv_c", [P, KVR // P], F32)

        self.PS = [self.es.enter_context(
            nc.psum_tensor(f"ps{i}", [P, 512], F32)) for i in range(4)]
        self.PR = [self.es.enter_context(
            nc.psum_tensor(f"pr{i}", [1, 512], F32)) for i in range(2)]
        self.PTx = [self.es.enter_context(
            nc.psum_tensor(f"pt{i}", [P, P], BF16)) for i in range(2)]

    def cast_to_dram(self, src, rows, cols, dst, stg, bnc, scale_col=None):
        p = self.p
        piece = 1024
        if not hasattr(self, "_castrel"):
            self._castrel = ([Z, Z], [Z, Z], [0])
        relS, relB, ii = self._castrel
        i = ii[0]
        for c in range(rows // P):
            for off in range(0, cols, piece):
                w = min(piece, cols - off)
                b = i % 2
                i += 1
                p.wait("sync", relS[b])
                dv = p.dma("sync", stg[:, b * 1024:b * 1024 + w],
                           src[c * P:(c + 1) * P, off:off + w], "dS")
                p.wait("vector", dv, relB[b])
                sap = stg[:, b * 1024:b * 1024 + w]
                bap = bnc[:, b * 1024:b * 1024 + w]
                if scale_col is None:
                    relS[b] = p.inc("vector", lambda e, o=bap, s=sap:
                                    e.tensor_copy(o, s), "vS")
                else:
                    sc = scale_col[:, c:c + 1]
                    relS[b] = p.inc("vector", lambda e, o=bap, s=sap, sc=sc:
                                    e.tensor_single_scalar(o, s, sc, ALU.mult),
                                    "vS")
                p.wait("gpsimd", relS[b])
                relB[b] = p.dma("gpsimd",
                                dst[c * P:(c + 1) * P, off:off + w], bap, "dG")
        ii[0] = i

    def phase0_and_wcast(self):
        nc, p = self.nc, self.p
        es0 = ExitStack()
        R = lambda nm, shp, dt: es0.enter_context(
            nc.sbuf_tensor(nm, shp, dt, side="right"))
        stg = R("stg0", [P, 2 * 1024], F32)
        bnc = R("bnc0", [P, 2 * 1024], BF16)
        k8 = R("k8", [NTT, T], BF16)
        for t_, v_ in ((self.ones_c, 1.0), (self.ones_r, 1.0)):
            p.run("vector", lambda e, t=t_, v=v_: e.memset(t[:], v))
        p.run("vector", lambda e: e.memset(self.klo_all[64:65, :], -1.0))
        for tt in range(NTT):
            p.dma("sync", self.xres[:, tt * D:(tt + 1) * D],
                  self.x0_d[tt * P:(tt + 1) * P, :], "dS")
        p.dma("sync", stg[:, 0:128], self.trim_d[:, :], "dS")
        p.dma("sync", stg[:, 128:256], self.iden_d[:, :], "dS")
        p.dma("sync", stg[:, 256:256 + QR // P], self.gqa_d[:, :], "dS")
        dv = p.dma("sync", stg[:, 280:280 + KVR // P], self.gkva_d[:, :], "dS")
        p.wait("vector", dv)
        p.inc("vector", lambda e: e.tensor_copy(self.trim_b[:], stg[:, 0:128]), "vS")
        p.inc("vector", lambda e: e.tensor_copy(self.iden_b[:], stg[:, 128:256]), "vS")
        p.inc("vector", lambda e: e.tensor_copy(self.iden_f[:], stg[:, 128:256]), "vS")
        p.inc("vector", lambda e: e.tensor_single_scalar(
            self.gqs_c[:], stg[:, 256:256 + QR // P], SCALE, ALU.mult), "vS")
        gv = p.inc("vector", lambda e: e.tensor_copy(
            self.gkv_c[:], stg[:, 280:280 + KVR // P]), "vS")
        for r in range(NC_):
            p.wait("sync", p.val("vS"), p.val("dG"))
            dvr = p.dma("sync", stg[0:NTT, 0:T],
                        self.krows_d[:, r * T:(r + 1) * T], "dS")
            p.wait("vector", dvr)
            cv = p.inc("vector", lambda e: e.tensor_copy(
                k8[:], stg[0:NTT, 0:T]), "vS")
            p.wait("gpsimd", cv)
            p.dma("gpsimd", self.klo_all[65:73, r * T:(r + 1) * T],
                  k8[:], "dG")
        p.wait("vector", gv)
        p.wait("sync", p.val("vS"))
        self.cast_to_dram(self.wqa_d, D, QR, self.wqaB, stg, bnc)
        self.cast_to_dram(self.wqb_d, QR, H * QKD, self.wqbB, stg, bnc,
                          scale_col=self.gqs_c)
        self.cast_to_dram(self.wkva_d, D, KVR + ROPE, self.wkvaB, stg, bnc)
        self.cast_to_dram(self.wkvb_d, KVR, H * (NOPE + VD), self.wkvbB,
                          stg, bnc, scale_col=self.gkv_c)
        self.cast_to_dram(self.wo_d, H * VD, D, self.woB, stg, bnc)
        p.barrier()
        es0.close()

    def phaseW(self):
        nc, p, st = self.nc, self.p, self.stats
        esW = ExitStack()
        R = lambda nm, shp, dt: esW.enter_context(
            nc.sbuf_tensor(nm, shp, dt, side="right"))
        wstage = R("wstage", [P, 2 * D], F32)
        q16b = R("q16b", [P, D], F16)
        weffb = R("weffb", [P, D], BF16)
        wTst = R("wTst", [P, 4 * P], BF16)
        wsrow = R("wsrow", [1, D], F32)
        relW = [Z, Z]
        relPT = [Z, Z]
        relWT = [Z] * 4
        relWS = Z
        for r in range(D // P):
            buf = r % 2
            p.wait("sync", relW[buf])
            dv = p.dma("sync", wstage[:, buf * D:(buf + 1) * D],
                       self.wffn_d[r * P:(r + 1) * P, :], "dS")
            p.wait("vector", dv)
            w = wstage[:, buf * D:(buf + 1) * D]
            v = p.inc("vector", lambda e, w=w: e.reduce_max(
                st[:, 0:1], w, axis=AX.X, apply_absolute_value=True), "vS")
            p.wait("vector", v)
            v = p.inc("vector", lambda e: e.tensor_scalar(
                st[:, 1:2], st[:, 0:1], 1e-8, 1.0 / 127.0,
                ALU.max, ALU.mult), "vS")
            p.wait("vector", v)
            v = p.inc("vector", lambda e: e.reciprocal(
                st[:, 3:4], st[:, 1:2]), "vS")
            p.wait("vector", v)
            p.inc("vector", lambda e, w=w: e.tensor_scalar(
                q16b[:], w, st[:, 3:4], MAGIC, ALU.mult, ALU.add), "vS")
            relW[buf] = p.inc("vector", lambda e: e.tensor_scalar(
                weffb[:], q16b[:], -MAGIC, 0.5, ALU.add, ALU.mult), "vS")
            wv = p.inc("vector", lambda e: e.tensor_copy(
                st[:, 4:5], st[:, 1:2]), "vS")
            p.wait("tensor", wv, relWS)
            tws = p.inc("tensor", lambda e: e.transpose(
                self.PS[2][0:1, 0:P], st[:, 4:5], self.iden_f[:]), "tS")
            p.wait("vector", tws)
            relWS = p.inc("vector", lambda e, r=r: e.tensor_copy(
                wsrow[0:1, r * P:(r + 1) * P], self.PS[2][0:1, 0:P]), "vS")
            for c in range(D // P):
                sl = c % 2
                p.wait("tensor", relW[buf], relPT[sl])
                tv = p.inc("tensor", lambda e, c=c, sl=sl: e.transpose(
                    self.PTx[sl][:, 0:P],
                    weffb[:, c * P:(c + 1) * P], self.iden_b[:]), "tS")
                sb = c % 4
                p.wait("vector", tv, relWT[sb])
                relPT[sl] = p.inc("vector", lambda e, sb=sb, sl=sl:
                                  e.tensor_copy(
                                      wTst[:, sb * P:(sb + 1) * P],
                                      self.PTx[sl][:, 0:P]), "vS")
                p.wait("gpsimd", relPT[sl])
                relWT[sb] = p.dma(
                    "gpsimd",
                    self.weffT_d[c * P:(c + 1) * P, r * P:(r + 1) * P],
                    wTst[:, sb * P:(sb + 1) * P], "dG")
        p.wait("sync", relWS)
        p.dma("sync", self.wsrowD[:, :], wsrow[:], "dS")
        p.barrier()
        esW.close()

    def ln_stats(self, src_fn, scratch_fn, ncols, eps):
        p, st = self.p, self.stats
        sv = None
        for tt in range(NTT):
            x = src_fn(tt)
            p.inc("vector", lambda e, x=x, tt=tt: e.reduce_sum(
                st[:, tt:tt + 1], x, axis=AX.X), "vS")
            sc = scratch_fn(tt)
            sv = p.inc("scalar", lambda e, x=x, tt=tt, sc=sc: e.activation(
                sc, x, AF.Square, accum_out=st[:, 8 + tt:9 + tt]), "sS")
        p.wait("vector", sv, p.val("vS"))
        v = p.inc("vector", lambda e: e.tensor_single_scalar(
            st[:, 16:24], st[:, 0:8], 1.0 / ncols, ALU.mult), "vS")
        p.wait("vector", v)
        v = p.inc("vector", lambda e: e.tensor_tensor(
            st[:, 24:32], st[:, 16:24], st[:, 16:24], ALU.mult), "vS")
        p.wait("vector", v)
        v = p.inc("vector", lambda e, n=ncols: e.scalar_tensor_tensor(
            st[:, 32:40], st[:, 8:16], 1.0 / n, st[:, 24:32],
            ALU.mult, ALU.subtract), "vS")
        p.wait("scalar", v)
        sv = p.inc("scalar", lambda e, eps=eps: e.activation(
            st[:, 40:48], st[:, 32:40], AF.Sqrt, bias=np.float32(eps)), "sS")
        p.wait("vector", sv)
        v = p.inc("vector", lambda e: e.reciprocal(
            st[:, 48:56], st[:, 40:48]), "vS")
        p.wait("vector", v)
        v = p.inc("vector", lambda e: e.scalar_tensor_tensor(
            st[:, 56:64], st[:, 16:24], -1.0, st[:, 48:56],
            ALU.mult, ALU.mult), "vS")
        p.wait("scalar", v)
        return v

    def ffn_phase(self, tag):
        nc, p, st = self.nc, self.p, self.stats
        esF = ExitStack()
        R = lambda nm, shp, dt: esF.enter_context(
            nc.sbuf_tensor(nm + tag, shp, dt, side="right"))
        xqT = R("xqT", [P, 16 * T], BF16)
        bias05 = R("bias05", [P, D], F32)
        wsb_bc = R("wsb_bc", [P, D], F32)
        esFa = ExitStack()
        Ra = lambda nm, shp, dt: esFa.enter_context(
            nc.sbuf_tensor(nm + tag, shp, dt, side="right"))
        lnf = Ra("lnf", [P, 2 * D], F32)
        q16 = Ra("q16f", [P, D], F16)
        xqb = Ra("xqb", [P, 2 * D], BF16)
        wrtmp = Ra("wrtmp", [1, D], F32)
        wrb = Ra("wrb", [1, 2 * D], BF16)

        for q in range(4):
            p.wait("sync", p.val("vS"))
            dvq = p.dma("sync", lnf[:, 0:512],
                        self.brep_d[:, q * 512:(q + 1) * 512], "dS")
            p.wait("vector", dvq)
            p.inc("vector", lambda e, q=q: e.tensor_single_scalar(
                bias05[:, q * 512:(q + 1) * 512], lnf[:, 0:512],
                0.5, ALU.mult), "vS")
        p.wait("sync", p.val("vS"))
        dvw = p.dma("sync", wrtmp[:, :], self.wsrowD[:, :], "dS")
        p.wait("vector", dvw)
        v = p.inc("vector", lambda e: e.tensor_copy(wrb[0:1, 0:D], wrtmp[:]),
                  "vS")
        p.wait("vector", v)
        v = p.inc("vector", lambda e: e.tensor_tensor(
            wrb[0:1, D:2 * D], wrtmp[:], wrb[0:1, 0:D], ALU.subtract), "vS")
        relBC = Z
        for oc in range(4):
            p.wait("tensor", v, relBC)
            p.run("tensor", lambda e, oc=oc: e.matmul(
                self.PS[2][:, 0:512], self.ones_r[:],
                wrb[0:1, oc * 512:(oc + 1) * 512], start=True, stop=False))
            tv = p.inc("tensor", lambda e, oc=oc: e.matmul(
                self.PS[2][:, 0:512], self.ones_r[:],
                wrb[0:1, D + oc * 512:D + (oc + 1) * 512],
                start=False, stop=True), "tS")
            p.wait("vector", tv)
            relBC = p.inc("vector", lambda e, oc=oc: e.tensor_copy(
                wsb_bc[:, oc * 512:(oc + 1) * 512], self.PS[2][:, 0:512]),
                "vS")
        p.wait("scalar", p.val("vS"))

        self.ln_stats(lambda tt: self.xres[:, tt * D:(tt + 1) * D],
                      lambda tt: lnf[:, (tt % 2) * D:(tt % 2 + 1) * D],
                      D, 1e-5)
        relLnV = [Z, Z]
        relPT = [Z, Z]
        relXqT = [Z, Z]
        for tt in range(NTT):
            sb = tt % 2
            x = self.xres[:, tt * D:(tt + 1) * D]
            ln = lnf[:, sb * D:(sb + 1) * D]
            p.wait("scalar", relLnV[sb])
            lv = p.inc("scalar", lambda e, x=x, tt=tt, ln=ln: e.activation(
                ln, x, AF.Identity, scale=st[:, 48 + tt:49 + tt],
                bias=st[:, 56 + tt:57 + tt]), "sS")
            p.wait("vector", lv)
            v = p.inc("vector", lambda e, ln=ln, tt=tt: e.reduce_max(
                st[:, tt:tt + 1], ln, axis=AX.X, apply_absolute_value=True),
                "vS")
            p.wait("vector", v)
            v = p.inc("vector", lambda e, tt=tt: e.tensor_scalar(
                self.xs_all[:, tt:tt + 1], st[:, tt:tt + 1], 1.0 / 127.0,
                1e-12, ALU.mult, ALU.max), "vS")
            p.wait("vector", v)
            v = p.inc("vector", lambda e, tt=tt: e.reciprocal(
                st[:, 8 + tt:9 + tt], self.xs_all[:, tt:tt + 1]), "vS")
            p.wait("vector", v)
            relLnV[sb] = p.inc("vector", lambda e, ln=ln, tt=tt:
                               e.tensor_scalar(q16[:], ln,
                                               st[:, 8 + tt:9 + tt], MAGIC,
                                               ALU.mult, ALU.add), "vS")
            p.wait("vector", relXqT[sb])
            xq = p.inc("vector", lambda e, sb=sb: e.tensor_single_scalar(
                xqb[:, sb * D:(sb + 1) * D], q16[:], -MAGIC, ALU.add), "vS")
            for c in range(D // P):
                sl = c % 2
                p.wait("tensor", xq, relPT[sl])
                tv = p.inc("tensor", lambda e, c=c, sl=sl, sb=sb: e.transpose(
                    self.PTx[sl][:, 0:P],
                    xqb[:, sb * D + c * P:sb * D + (c + 1) * P],
                    self.iden_b[:]), "tS")
                if c == D // P - 1:
                    relXqT[sb] = tv
                p.wait("vector", tv)
                relPT[sl] = p.inc(
                    "vector", lambda e, c=c, tt=tt, sl=sl: e.tensor_copy(
                        xqT[:, c * T + tt * P:c * T + (tt + 1) * P],
                        self.PTx[sl][:, 0:P]), "vS")
        xqT_rdy = p.val("vS")
        esFa.close()
        wslab = R("wslab", [P, 2 * 16 * 512], BF16)
        usb = R("usb", [P, 2 * 512], F32)

        relSlab = [Z, Z]
        relPS = [Z, Z]
        relU = [Z, Z]
        for oc in range(4):
            wb = oc % 2
            p.wait("sync", relSlab[wb], xqT_rdy)
            dv = p.dma(
                "sync",
                A(wslab, wb * 16 * 512, [2 * 16 * 512, P], [512, 16], [1, 512]),
                A(self.weffT_d, oc * 512, [D, P], [P * D, 16], [1, 512]),
                "dS")
            for tt in range(NTT):
                pb = (oc * NTT + tt) % 2
                p.wait("tensor", dv, xqT_rdy, relPS[pb])
                for c in range(16):
                    fn = (lambda e, c=c, tt=tt, wb=wb, pb=pb:
                          e.matmul(self.PS[pb][:, 0:512],
                                   xqT[:, c * T + tt * P:c * T + (tt + 1) * P],
                                   wslab[:, wb * 16 * 512 + c * 512:
                                         wb * 16 * 512 + (c + 1) * 512],
                                   start=(c == 0), stop=(c == 15)))
                    if c == 15:
                        mv = p.inc("tensor", fn, "tS")
                    else:
                        p.run("tensor", fn)
                relSlab[wb] = mv
                p.wait("vector", mv, relU[pb])
                p.inc("vector", lambda e, pb=pb, tt=tt, oc=oc:
                      e.scalar_tensor_tensor(
                          usb[:, pb * 512:(pb + 1) * 512],
                          self.PS[pb][:, 0:512], self.xs_all[:, tt:tt + 1],
                          wsb_bc[:, oc * 512:(oc + 1) * 512],
                          ALU.mult, ALU.mult), "vS")
                relPS[pb] = p.val("vS")
                p.inc("vector", lambda e, pb=pb, oc=oc: e.tensor_tensor(
                    usb[:, pb * 512:(pb + 1) * 512],
                    usb[:, pb * 512:(pb + 1) * 512],
                    bias05[:, oc * 512:(oc + 1) * 512], ALU.add), "vS")
                xsl = self.xres[:, tt * D + oc * 512:tt * D + (oc + 1) * 512]
                relU[pb] = p.inc(
                    "vector", lambda e, x=xsl, pb=pb: e.tensor_tensor(
                        x, x, usb[:, pb * 512:(pb + 1) * 512], ALU.add), "vS")
        p.barrier()
        esF.close()

    def kv_q_phases(self):
        import os
        CUT = os.environ.get("KVQ_CUT", "")
        nc, p, st = self.nc, self.p, self.stats
        esM = ExitStack()
        R = lambda nm, shp, dt: esM.enter_context(
            nc.sbuf_tensor(nm, shp, dt, side="right"))
        qcsT = R("qcsT", [P, 6 * T], BF16)
        cosb = R("cosb", [P, NTT * ROPE], F32)
        snb = R("snb", [P, NTT * ROPE], F32)
        lnscr = R("lnscr", [P, D], F32)
        esL = ExitStack()
        ln2T = esL.enter_context(
            nc.sbuf_tensor("ln2T", [P, 16 * T], BF16, side="right"))
        ln2b = esL.enter_context(
            nc.sbuf_tensor("ln2b", [P, 2 * D], BF16, side="right"))
        self.ln2T_h = ln2T
        p.dma("sync",
              A(cosb, 0, [NTT * ROPE, P], [ROPE, NTT], [1, ROPE]),
              A(self.cos_d, 0, [ROPE, P], [P * ROPE, NTT], [1, ROPE]), "dS")
        cosrdy = p.dma("sync",
                       A(snb, 0, [NTT * ROPE, P], [ROPE, NTT], [1, ROPE]),
                       A(self.sn_d, 0, [ROPE, P], [P * ROPE, NTT], [1, ROPE]),
                       "dS")

        self.ln_stats(lambda tt: self.xres[:, tt * D:(tt + 1) * D],
                      lambda tt: lnscr[:, 0:D], D, 1e-5)
        relLn2 = [Z, Z]
        relPT = [Z, Z]
        for tt in range(NTT):
            sb = tt % 2
            x = self.xres[:, tt * D:(tt + 1) * D]
            ln = ln2b[:, sb * D:(sb + 1) * D]
            p.wait("scalar", relLn2[sb])
            lv = p.inc("scalar", lambda e, x=x, tt=tt, ln=ln: e.activation(
                ln, x, AF.Identity, scale=st[:, 48 + tt:49 + tt],
                bias=st[:, 56 + tt:57 + tt]), "sS")
            for c in range(D // P):
                sl = c % 2
                p.wait("tensor", lv, relPT[sl])
                tv = p.inc("tensor", lambda e, c=c, sl=sl, ln=ln: e.transpose(
                    self.PTx[sl][:, 0:P],
                    ln[:, c * P:(c + 1) * P], self.iden_b[:]), "tS")
                if c == D // P - 1:
                    relLn2[sb] = tv
                p.wait("vector", tv)
                relPT[sl] = p.inc(
                    "vector", lambda e, c=c, tt=tt, sl=sl: e.tensor_copy(
                        ln2T[:, c * T + tt * P:c * T + (tt + 1) * P],
                        self.PTx[sl][:, 0:P]), "vS")
        ln2T_rdy = p.val("vS")
        if CUT == "ln2":
            p.barrier()
            esL.close()
            esM.close()
            return

        esK = ExitStack()
        RK = lambda nm, shp, dt: esK.enter_context(
            nc.sbuf_tensor(nm, shp, dt, side="right"))
        wkva_sb = RK("wkva_sb", [P, 16 * (KVR + ROPE)], BF16)
        kvbsl = RK("kvbsl", [P, 2 * 4 * 512], BF16)
        kvc = RK("kvc", [P, 2 * KVR], F32)
        rkv = RK("rkv", [P, 2 * KVR], BF16)
        rkvT = RK("rkvT", [P, 4 * T], BF16)
        kvexp = RK("kvexp", [P, 2 * 4096], BF16)
        krall = RK("krall", [P, NTT * ROPE], BF16)
        ropet = RK("ropet", [P, 2 * ROPE], F32)
        kTst = RK("kTst", [P, 4 * P], BF16)
        knall = RK("knall", [P, P], F32)
        krn = RK("krn", [P, NTT], F32)
        knsc = RK("knsc", [P, KVR], BF16)

        wkva_rdy = p.dma(
            "sync",
            A(wkva_sb, 0, [16 * (KVR + ROPE), P], [KVR + ROPE, 16],
              [1, KVR + ROPE]),
            A(self.wkvaB, 0, [KVR + ROPE, P], [P * (KVR + ROPE), 16],
              [1, KVR + ROPE]), "dS")

        relPSa = [Z, Z]
        relPSb = [Z, Z]
        relKvcS = [Z, Z]
        relRkvT = [Z, Z]
        relPT = [Z, Z]
        for tt in range(NTT):
            sb = tt % 2
            pa, pb_ = sb, 2 + sb
            p.wait("tensor", ln2T_rdy, wkva_rdy, relPSa[sb], relPSb[sb])
            for c in range(16):
                fa = (lambda e, c=c, tt=tt, pa=pa: e.matmul(
                    self.PS[pa][:, 0:512],
                    ln2T[:, c * T + tt * P:c * T + (tt + 1) * P],
                    wkva_sb[:, c * (KVR + ROPE):c * (KVR + ROPE) + KVR],
                    start=(c == 0), stop=(c == 15)))
                fb = (lambda e, c=c, tt=tt, pb_=pb_: e.matmul(
                    self.PS[pb_][:, 0:ROPE],
                    ln2T[:, c * T + tt * P:c * T + (tt + 1) * P],
                    wkva_sb[:, c * (KVR + ROPE) + KVR:(c + 1) * (KVR + ROPE)],
                    start=(c == 0), stop=(c == 15)))
                if c == 15:
                    mva = p.inc("tensor", fa, "tS")
                    mvb = p.inc("tensor", fb, "tS")
                else:
                    p.run("tensor", fa)
                    p.run("tensor", fb)
            p.wait("vector", mva, relKvcS[sb])
            cv = p.inc("vector", lambda e, pa=pa, sb=sb: e.tensor_copy(
                kvc[:, sb * KVR:(sb + 1) * KVR], self.PS[pa][:, 0:512]), "vS")
            relPSa[sb] = cv
            co = cosb[:, tt * ROPE:(tt + 1) * ROPE]
            sn = snb[:, tt * ROPE:(tt + 1) * ROPE]
            rt, rs = ropet[:, 0:ROPE], ropet[:, ROPE:2 * ROPE]
            p.wait("vector", mvb, cosrdy)
            p.inc("vector", lambda e, pb_=pb_, co=co, rt=rt: e.tensor_tensor(
                rt, self.PS[pb_][:, 0:ROPE], co, ALU.mult), "vS")
            p.inc("vector", lambda e, pb_=pb_, sn=sn, rs=rs: e.tensor_tensor(
                rs[:, 0:32], self.PS[pb_][:, 32:64], sn[:, 0:32], ALU.mult),
                "vS")
            v = p.inc("vector", lambda e, pb_=pb_, sn=sn, rs=rs:
                      e.tensor_tensor(rs[:, 32:64], self.PS[pb_][:, 0:32],
                                      sn[:, 32:64], ALU.mult), "vS")
            relPSb[sb] = v
            p.wait("vector", v)
            krv = p.inc("vector", lambda e, tt=tt, rt=rt, rs=rs:
                        e.tensor_tensor(krall[:, tt * ROPE:(tt + 1) * ROPE],
                                        rt, rs, ALU.add), "vS")
            p.wait("scalar", krv)
            p.inc("scalar", lambda e, tt=tt: e.activation(
                knsc[:, 0:ROPE], krall[:, tt * ROPE:(tt + 1) * ROPE],
                AF.Square, accum_out=krn[:, tt:tt + 1]), "sS")
            p.wait("scalar", cv)
            sv = p.inc("scalar", lambda e, sb=sb: e.activation(
                lnscr[:, 0:KVR], kvc[:, sb * KVR:(sb + 1) * KVR],
                AF.Square, accum_out=st[:, 32:33]), "sS")
            p.wait("scalar", sv)
            sv = p.inc("scalar", lambda e: e.activation(
                st[:, 40:41], st[:, 32:33], AF.Sqrt,
                scale=np.float32(1.0 / KVR), bias=np.float32(1e-6)), "sS")
            p.wait("vector", sv)
            v = p.inc("vector", lambda e: e.reciprocal(
                st[:, 48:49], st[:, 40:41]), "vS")
            p.wait("scalar", v, relRkvT[sb])
            rv = p.inc("scalar", lambda e, sb=sb: e.activation(
                rkv[:, sb * KVR:(sb + 1) * KVR],
                kvc[:, sb * KVR:(sb + 1) * KVR],
                AF.Identity, scale=st[:, 48:49]), "sS")
            relKvcS[sb] = rv
            for c in range(4):
                sl = c % 2
                p.wait("tensor", rv, relPT[sl])
                tv = p.inc("tensor", lambda e, c=c, sl=sl, sb=sb: e.transpose(
                    self.PTx[sl][:, 0:P],
                    rkv[:, sb * KVR + c * P:sb * KVR + (c + 1) * P],
                    self.iden_b[:]), "tS")
                if c == 3:
                    relRkvT[sb] = tv
                p.wait("vector", tv)
                relPT[sl] = p.inc(
                    "vector", lambda e, c=c, tt=tt, sl=sl: e.tensor_copy(
                        rkvT[:, c * T + tt * P:c * T + (tt + 1) * P],
                        self.PTx[sl][:, 0:P]), "vS")
        rkvT_rdy = p.val("vS")
        if CUT == "kv":
            p.barrier()
            esK.close()
            esL.close()
            esM.close()
            return

        relExpV = [(Z,), (Z,)]
        relPSe = [Z, Z]
        relKT = [Z] * 4
        relPT = [Z, Z]
        relBsl = [Z, Z]
        for tt in range(NTT):
            eb = tt % 2
            p.wait("vector", *relExpV[eb])
            lastpub = Z
            sv = Z
            tv = Z
            for oc in range(8):
                pb = oc % 2
                bslb = oc % 2
                p.wait("sync", relBsl[bslb])
                bv = p.dma(
                    "sync",
                    A(kvbsl, bslb * 4 * 512, [2 * 4 * 512, P], [512, 4],
                      [1, 512]),
                    A(self.wkvbB, oc * 512, [H * (NOPE + VD), P],
                      [P * H * (NOPE + VD), 4], [1, 512]), "dS")
                p.wait("tensor", rkvT_rdy, bv, relPSe[pb])
                for c in range(4):
                    fn = (lambda e, c=c, tt=tt, oc=oc, pb=pb, bslb=bslb:
                          e.matmul(
                              self.PS[pb][:, 0:512],
                              rkvT[:, c * T + tt * P:c * T + (tt + 1) * P],
                              kvbsl[:, bslb * 4 * 512 + c * 512:
                                    bslb * 4 * 512 + (c + 1) * 512],
                              start=(c == 0), stop=(c == 3)))
                    if c == 3:
                        mv = p.inc("tensor", fn, "tS")
                    else:
                        p.run("tensor", fn)
                relBsl[bslb] = mv
                p.wait("vector", mv)
                relPSe[pb] = p.inc(
                    "vector", lambda e, pb=pb, eb=eb, oc=oc: e.tensor_copy(
                        kvexp[:, eb * 4096 + oc * 512:
                              eb * 4096 + (oc + 1) * 512],
                        self.PS[pb][:, 0:512]), "vS")
            kvexp_rdy = p.val("vS")
            for h in range(H):
                vsl = kvexp[:, eb * 4096 + h * 256 + NOPE:
                            eb * 4096 + (h + 1) * 256]
                p.wait("gpsimd", kvexp_rdy)
                lastpub = p.dma(
                    "gpsimd",
                    A(self.kvown, V_OFF + h * T * VD + tt * P * VD,
                      [VD, P], [1, VD]),
                    vsl, "dG")
                nsl = kvexp[:, eb * 4096 + h * 256:eb * 4096 + h * 256 + NOPE]
                sl = h % 2
                p.wait("tensor", kvexp_rdy, relPT[sl])
                tv = p.inc("tensor", lambda e, nsl=nsl, sl=sl: e.transpose(
                    self.PTx[sl][:, 0:P], nsl, self.iden_b[:]),
                    "tS")
                kb = h % 4
                p.wait("vector", tv, relKT[kb])
                relPT[sl] = p.inc("vector", lambda e, kb=kb, sl=sl:
                                  e.tensor_copy(
                                      kTst[:, kb * P:(kb + 1) * P],
                                      self.PTx[sl][:, 0:P]), "vS")
                p.wait("gpsimd", relPT[sl])
                relKT[kb] = p.dma(
                    "gpsimd",
                    A(self.kvown, KHI_OFF + h * P * T + tt * P, [T, P], [1, P]),
                    kTst[:, kb * P:(kb + 1) * P], "dG")
                p.wait("scalar", kvexp_rdy)
                sv = p.inc("scalar", lambda e, nsl=nsl, h=h, tt=tt:
                           e.activation(knsc[:, 0:NOPE], nsl, AF.Square,
                                        accum_out=knall[:, h * 8 + tt:
                                                        h * 8 + tt + 1]),
                           "sS")
            relExpV[eb] = (lastpub, sv, tv)
        for tt in range(NTT):
            sl = tt % 2
            p.wait("tensor", p.val("vS"), relPT[sl])
            tv = p.inc("tensor", lambda e, tt=tt, sl=sl: e.transpose(
                self.PTx[sl][0:ROPE, 0:P],
                krall[:, tt * ROPE:(tt + 1) * ROPE], self.iden_b[:]), "tS")
            kb = tt % 4
            p.wait("vector", tv, relKT[kb])
            relPT[sl] = p.inc("vector", lambda e, kb=kb, sl=sl: e.tensor_copy(
                kTst[0:ROPE, kb * P:(kb + 1) * P],
                self.PTx[sl][0:ROPE, 0:P]), "vS")
            p.wait("gpsimd", relPT[sl])
            relKT[kb] = p.dma(
                "gpsimd",
                A(self.kvown, KLO_OFF + tt * P, [T, ROPE], [1, P]),
                kTst[0:ROPE, kb * P:(kb + 1) * P], "dG")
        p.wait("vector", p.val("sS"))
        for tt in range(NTT):
            p.inc("vector", lambda e, tt=tt: e.tensor_single_scalar(
                A(knall, tt, [P, P], [8, H]), A(knall, tt, [P, P], [8, H]),
                krn[:, tt:tt + 1], ALU.add), "vS")
        p.wait("tensor", p.val("vS"))
        tv = p.inc("tensor", lambda e: e.transpose(
            self.PS[0][:, 0:128], knall[:], self.iden_f[:]), "tS")
        p.wait("vector", tv)
        v = p.inc("vector", lambda e: e.reduce_max(
            st[:, 0:1], self.PS[0][:, 0:128], axis=AX.X), "vS")
        p.wait("tensor", v)
        tv = p.inc("tensor", lambda e: e.transpose(
            self.PS[1][0:1, 0:128], st[:, 0:1], self.iden_f[:]), "tS")
        p.wait("vector", tv)
        v = p.inc("vector", lambda e: e.reduce_max(
            A(st, 32, [64, 1], [1, H]),
            A(self.PS[1], 0, [512, 1], [8, H], [1, NTT]), axis=AX.X), "vS")
        p.wait("vector", v)
        p.inc("vector", lambda e: e.memset(self.knrow_b[:], 0.0), "vS")
        v = p.inc("vector", lambda e: e.tensor_copy(
            self.knrow_b[0:1, 0:H], A(st, 32, [64, 1], [1, H])), "vS")
        p.wait("gpsimd", v)
        p.dma("gpsimd", A(self.kvown, KN_OFF, [P, 1], [1, P]),
              self.knrow_b[0:1, 0:P], "dG")
        if CUT == "pub":
            p.barrier()
            esK.close()
            esL.close()
            esM.close()
            return
        p.wait("gpsimd", p.val("dG"))
        ccv = p.inc("gpsimd", lambda e: e.collective_compute(
            "AllGather", ALU.bypass, replica_groups=[list(range(NC_))],
            ins=[self.kvown.ap()], outs=[self.kvall.ap()]), "cc")
        esK.close()
        if CUT == "ag":
            p.barrier()
            esL.close()
            esM.close()
            return

        esQ = ExitStack()
        RQ = lambda nm, shp, dt: esQ.enter_context(
            nc.sbuf_tensor(nm, shp, dt, side="right"))
        wqa_sb = RQ("wqa_sb", [P, 16 * QR], BF16)
        qcf = RQ("qcf", [P, 2 * QR], F32)
        qcs = RQ("qcs", [P, 2 * QR], BF16)
        p.wait("sync", p.val("dG"), p.val("tS"), p.val("vS"), p.val("sS"))
        wqa_rdy = p.dma(
            "sync",
            A(wqa_sb, 0, [16 * QR, P], [QR, 16], [1, QR]),
            A(self.wqaB, 0, [QR, P], [P * QR, 16], [1, QR]), "dS")
        relPSa = [Z, Z]
        relPSb = [Z, Z]
        relQcS = [Z, Z]
        relQcsT = [Z, Z]
        relPT = [Z, Z]
        for tt in range(NTT):
            sb = tt % 2
            pa, pb_ = sb, 2 + sb
            p.wait("tensor", ln2T_rdy, wqa_rdy, relPSa[sb], relPSb[sb])
            for c in range(16):
                fa = (lambda e, c=c, tt=tt, pa=pa: e.matmul(
                    self.PS[pa][:, 0:512],
                    ln2T[:, c * T + tt * P:c * T + (tt + 1) * P],
                    wqa_sb[:, c * QR:c * QR + 512],
                    start=(c == 0), stop=(c == 15)))
                fb = (lambda e, c=c, tt=tt, pb_=pb_: e.matmul(
                    self.PS[pb_][:, 0:256],
                    ln2T[:, c * T + tt * P:c * T + (tt + 1) * P],
                    wqa_sb[:, c * QR + 512:(c + 1) * QR],
                    start=(c == 0), stop=(c == 15)))
                if c == 15:
                    mva = p.inc("tensor", fa, "tS")
                    mvb = p.inc("tensor", fb, "tS")
                else:
                    p.run("tensor", fa)
                    p.run("tensor", fb)
            p.wait("vector", mva, relQcS[sb])
            relPSa[sb] = p.inc("vector", lambda e, pa=pa, sb=sb: e.tensor_copy(
                qcf[:, sb * QR:sb * QR + 512], self.PS[pa][:, 0:512]), "vS")
            p.wait("vector", mvb)
            cv = p.inc("vector", lambda e, pb_=pb_, sb=sb: e.tensor_copy(
                qcf[:, sb * QR + 512:(sb + 1) * QR], self.PS[pb_][:, 0:256]),
                "vS")
            relPSb[sb] = cv
            p.wait("scalar", cv)
            sv = p.inc("scalar", lambda e, sb=sb: e.activation(
                lnscr[:, 0:QR], qcf[:, sb * QR:(sb + 1) * QR], AF.Square,
                accum_out=st[:, 32:33]), "sS")
            p.wait("scalar", sv)
            sv = p.inc("scalar", lambda e: e.activation(
                st[:, 40:41], st[:, 32:33], AF.Sqrt,
                scale=np.float32(1.0 / QR), bias=np.float32(1e-6)), "sS")
            p.wait("vector", sv)
            v = p.inc("vector", lambda e: e.reciprocal(
                st[:, 48:49], st[:, 40:41]), "vS")
            p.wait("scalar", v, relQcsT[sb])
            rv = p.inc("scalar", lambda e, sb=sb: e.activation(
                qcs[:, sb * QR:(sb + 1) * QR],
                qcf[:, sb * QR:(sb + 1) * QR],
                AF.Identity, scale=st[:, 48:49]), "sS")
            relQcS[sb] = rv
            for c in range(6):
                sl = c % 2
                p.wait("tensor", rv, relPT[sl])
                tv = p.inc("tensor", lambda e, c=c, sl=sl, sb=sb: e.transpose(
                    self.PTx[sl][:, 0:P],
                    qcs[:, sb * QR + c * P:sb * QR + (c + 1) * P],
                    self.iden_b[:]), "tS")
                if c == 5:
                    relQcsT[sb] = tv
                p.wait("vector", tv)
                relPT[sl] = p.inc(
                    "vector", lambda e, c=c, tt=tt, sl=sl: e.tensor_copy(
                        qcsT[:, c * T + tt * P:c * T + (tt + 1) * P],
                        self.PTx[sl][:, 0:P]), "vS")
        qcsT_rdy = p.val("vS")
        esQ.close()
        esL.close()
        if CUT == "qc":
            p.barrier()
            esM.close()
            return

        esQ2 = ExitStack()
        RQ2 = lambda nm, shp, dt: esQ2.enter_context(
            nc.sbuf_tensor(nm, shp, dt, side="right"))
        wqb_sb = RQ2("wqb_sb", [P, 6 * H * QKD], BF16)
        stn = RQ2("stn", [P, P], BF16)
        sto = RQ2("sto", [P, P], BF16)
        qTst = RQ2("qTst", [P, 2 * P], BF16)
        qTst2 = RQ2("qTst2", [65, 2 * P], BF16)
        rop2 = RQ2("rop2", [P, 2 * ROPE], F32)
        p.wait("sync", qcsT_rdy, p.val("tS"))
        wqb_rdy = p.dma(
            "sync",
            A(wqb_sb, 0, [6 * H * QKD, P], [H * QKD, 6], [1, H * QKD]),
            A(self.wqbB, 0, [H * QKD, P], [P * H * QKD, 6], [1, H * QKD]),
            "dS")
        p.run("vector", lambda e: e.memset(sto[:, 65:P], 0.0))
        p.wait("gpsimd", ccv)
        gv = p.dma("gpsimd", A(lnscr, 0, [D, 1], [16, NC_], [1, H]),
                   A(self.kvall, KN_OFF, [SZKV, 1], [SZKV, NC_], [1, H]),
                   "dG")
        p.wait("vector", gv)
        v = p.inc("vector", lambda e: e.reduce_max(
            A(st, 32, [64, 1], [1, H]),
            A(lnscr, 0, [D, 1], [1, H], [16, NC_]), axis=AX.X), "vS")
        p.wait("vector", v)
        v = p.inc("vector", lambda e: e.tensor_copy(
            self.knrow_b[0:1, 0:H], A(st, 32, [64, 1], [1, H])), "vS")
        p.wait("tensor", v, p.val("vS"))
        tv = p.inc("tensor", lambda e: e.matmul(
            self.PS[2][:, 0:H], self.ones_r[:], self.knrow_b[0:1, 0:H],
            start=True, stop=True), "tS")
        p.wait("vector", tv)
        kmx = p.inc("vector", lambda e: e.tensor_copy(
            self.kmaxbc[:], self.PS[2][:, 0:H]), "vS")

        relPS = [Z, Z]
        relStn = Z
        relSto = Z
        relQT = [Z, Z]
        relQT2 = [Z, Z]
        relPT = [Z, Z]
        for h in range(H):
            for tt in range(NTT):
                pq = (h * NTT + tt) % 2
                p.wait("tensor", qcsT_rdy, wqb_rdy, relPS[pq])
                for c in range(6):
                    fn = (lambda e, c=c, tt=tt, h=h, pq=pq: e.matmul(
                        self.PS[pq][:, 0:QKD],
                        qcsT[:, c * T + tt * P:c * T + (tt + 1) * P],
                        wqb_sb[:, c * H * QKD + h * QKD:
                               c * H * QKD + (h + 1) * QKD],
                        start=(c == 0), stop=(c == 5)))
                    if c == 5:
                        mv = p.inc("tensor", fn, "tS")
                    else:
                        p.run("tensor", fn)
                p.wait("vector", mv, relStn, cosrdy)
                p.inc("vector", lambda e, pq=pq: e.tensor_copy(
                    stn[:], self.PS[pq][:, 0:NOPE]), "vS")
                co = cosb[:, tt * ROPE:(tt + 1) * ROPE]
                sn = snb[:, tt * ROPE:(tt + 1) * ROPE]
                rt, rs = rop2[:, 0:ROPE], rop2[:, ROPE:2 * ROPE]
                p.inc("vector", lambda e, pq=pq, co=co, rt=rt: e.tensor_tensor(
                    rt, self.PS[pq][:, NOPE:QKD], co, ALU.mult), "vS")
                p.inc("vector", lambda e, pq=pq, sn=sn, rs=rs: e.tensor_tensor(
                    rs[:, 0:32], self.PS[pq][:, NOPE + 32:QKD], sn[:, 0:32],
                    ALU.mult), "vS")
                v = p.inc("vector", lambda e, pq=pq, sn=sn, rs=rs:
                          e.tensor_tensor(rs[:, 32:64],
                                          self.PS[pq][:, NOPE:NOPE + 32],
                                          sn[:, 32:64], ALU.mult), "vS")
                p.wait("scalar", v, kmx)
                sv = p.inc("scalar", lambda e, pq=pq: e.activation(
                    lnscr[:, 0:QKD], self.PS[pq][:, 0:QKD], AF.Square,
                    accum_out=st[:, 33:34]), "sS")
                relPS[pq] = sv
                p.wait("scalar", sv)
                bv = p.inc("scalar", lambda e, h=h: e.activation(
                    st[:, 41:42], st[:, 33:34], AF.Sqrt,
                    scale=self.kmaxbc[:, h:h + 1]), "sS")
                p.wait("vector", v, bv, relSto)
                p.inc("vector", lambda e, rt=rt, rs=rs: e.tensor_tensor(
                    sto[:, 0:ROPE], rt, rs, ALU.add), "vS")
                v = p.inc("vector", lambda e: e.tensor_copy(
                    sto[:, ROPE:ROPE + 1], st[:, 41:42]), "vS")
                sl = 0
                p.wait("tensor", p.val("vS"), relPT[sl])
                tv = p.inc("tensor", lambda e, sl=sl: e.transpose(
                    self.PTx[sl][:, 0:P], stn[:], self.iden_b[:]),
                    "tS")
                relStn = tv
                qb = (h * NTT + tt) % 2
                p.wait("vector", tv, relQT[qb])
                relPT[sl] = p.inc("vector", lambda e, qb=qb, sl=sl:
                                  e.tensor_copy(qTst[:, qb * P:(qb + 1) * P],
                                                self.PTx[sl][:, 0:P]),
                                  "vS")
                p.wait("gpsimd", relPT[sl])
                relQT[qb] = p.dma(
                    "gpsimd",
                    A(self.qhi_dd, h * NOPE * T + tt * P, [T, P], [1, P]),
                    qTst[:, qb * P:(qb + 1) * P], "dG")
                sl2 = 1
                p.wait("tensor", relPT[sl2])
                tv = p.inc("tensor", lambda e, sl2=sl2: e.transpose(
                    self.PTx[sl2][:, 0:P], sto[:],
                    self.iden_b[:]), "tS")
                relSto = tv
                p.wait("vector", tv, relQT2[qb])
                relPT[sl2] = p.inc(
                    "vector", lambda e, qb=qb, sl2=sl2: e.tensor_copy(
                        qTst2[0:65, qb * P:(qb + 1) * P],
                        self.PTx[sl2][0:65, 0:P]), "vS")
                p.wait("gpsimd", relPT[sl2])
                relQT2[qb] = p.dma(
                    "gpsimd",
                    A(self.qlo_dd, h * 65 * T + tt * P, [T, 65], [1, P]),
                    qTst2[0:65, qb * P:(qb + 1) * P], "dG")
        p.barrier()
        esQ2.close()
        esM.close()

    def attention_phase(self):
        nc, p = self.nc, self.p
        self.esA = ExitStack()
        self.attn_fm = self.esA.enter_context(
            nc.sbuf_tensor("attn_fm", [P, H * T], BF16, side="right"))
        esA2 = ExitStack()
        R2 = lambda nm, shp, dt: esA2.enter_context(
            nc.sbuf_tensor(nm, shp, dt, side="right"))
        khi_st = R2("khi_st", [P, 2 * T], BF16)
        v_st = R2("v_st", [P, 2 * T], BF16)
        khi_own = R2("khi_own", [P, 2 * T], BF16)
        v_own = R2("v_own", [P, 2 * T], BF16)
        qhi_st = R2("qhi_st", [P, 2 * T], BF16)
        qlo_st = R2("qlo_st", [73, 2 * T], BF16)
        klo_own = R2("klo_own", [65, T], BF16)
        qflag_b = R2("qflag_b", [NTT, T], BF16)
        fstg = R2("fstg", [NTT, T], F32)
        probs = R2("probs", [P, 3 * 512], BF16)
        probsd = R2("probsd", [P, 2 * P], BF16)
        pdm = R2("pdm", [P, 2 * P], BF16)
        rsf = R2("rsf", [1, 512], F32)
        rhib = R2("rhib", [1, 1024], BF16)
        bcsb = R2("bcsb", [P, 512], F32)

        p.run("vector", lambda e: e.memset(klo_own[64:65, :], -1.0))
        dvf = p.dma("sync", fstg[:], self.qflag_d[:, :], "dS")
        p.wait("vector", dvf)
        cv = p.inc("vector", lambda e: e.tensor_copy(qflag_b[:], fstg[:]),
                   "vS")
        lv = p.dma("gpsimd", klo_own[0:ROPE, :],
                   A(self.kvown, KLO_OFF, [T, ROPE], [1, T]), "dG")
        for r in range(NC_):
            lv = p.dma("gpsimd",
                       self.klo_all[0:ROPE, r * T:(r + 1) * T],
                       A(self.kvall, r * SZKV + KLO_OFF, [T, ROPE], [1, T]),
                       "dG")
        p.wait("gpsimd", cv, p.val("vS"))
        for b in range(2):
            lv = p.dma("gpsimd", qlo_st[65:73, b * T:(b + 1) * T],
                       qflag_b[:], "dG")

        relQ = [Z, Z]
        relOwn = [Z, Z]
        relKhi = [Z, Z]
        relV = [Z, Z]
        relProbs = [Z, Z, Z]
        relProbsD = [Z, Z]
        relPdm = [Z, Z]
        relSc = [Z, Z]
        relPV = [Z, Z]
        relSum = [Z, Z]
        relBc = Z
        dkh = [Z, Z]
        dvv = [Z, Z]
        pc = 0
        sc = 0
        for h in range(H):
            hb = h % 2
            p.wait("sync", relQ[hb])
            p.dma("sync", qhi_st[:, hb * T:(hb + 1) * T],
                  A(self.qhi_dd, h * NOPE * T, [T, P], [1, T]), "dS")
            dq2 = p.dma("sync", qlo_st[0:65, hb * T:(hb + 1) * T],
                        A(self.qlo_dd, h * 65 * T, [T, 65], [1, T]), "dS")
            p.wait("gpsimd", relOwn[hb])
            p.dma("gpsimd", khi_own[:, hb * T:(hb + 1) * T],
                  A(self.kvown, KHI_OFF + h * P * T, [T, P], [1, T]), "dG")
            dow2 = p.dma("gpsimd",
                         A(v_own, hb * T, [2 * T, P], [P, NTT], [1, P]),
                         A(self.kvown, V_OFF + h * T * VD,
                           [VD, P], [P * VD, NTT], [1, VD]), "dG")
            for r in range(NC_):
                kb = (h * NC_ + r) % 2
                p.wait("sync", relKhi[kb])
                dkh[kb] = p.dma(
                    "sync", khi_st[:, kb * T:(kb + 1) * T],
                    A(self.kvall, r * SZKV + KHI_OFF + h * P * T,
                      [T, P], [1, T]), "dS")
                p.wait("gpsimd", relV[kb])
                dvv[kb] = p.dma(
                    "gpsimd", A(v_st, kb * T, [2 * T, P], [P, NTT], [1, P]),
                    A(self.kvall, r * SZKV + V_OFF + h * T * VD,
                      [VD, P], [P * VD, NTT], [1, VD]), "dG")
                for kc in range(NTT):
                    for qt in range(2):
                        s_ = sc % 2
                        sc += 1
                        first = (r == 0 and kc == 0)
                        p.wait("tensor", dkh[kb], dq2, lv, relSc[s_])
                        if first:
                            p.wait("tensor", relPV[qt], relSum[qt])
                        p.run("tensor",
                              lambda e, kb=kb, kc=kc, qt=qt, s_=s_, hb=hb:
                              e.matmul(self.PS[s_][:, 0:512],
                                       khi_st[:, kb * T + kc * P:
                                              kb * T + (kc + 1) * P],
                                       qhi_st[:, hb * T + qt * 512:
                                              hb * T + (qt + 1) * 512],
                                       start=True, stop=False))
                        mv = p.inc("tensor",
                                   lambda e, r=r, kc=kc, qt=qt, s_=s_, hb=hb:
                                   e.matmul(
                                       self.PS[s_][:, 0:512],
                                       self.klo_all[0:73,
                                                    r * T + kc * P:
                                                    r * T + (kc + 1) * P],
                                       qlo_st[0:73,
                                              hb * T + qt * 512:
                                              hb * T + (qt + 1) * 512],
                                       start=False, stop=True), "tS")
                        pc_ = pc % 3
                        pc += 1
                        p.wait("scalar", mv, relProbs[pc_])
                        ev = p.inc("scalar", lambda e, s_=s_, pc_=pc_:
                                   e.activation(
                                       probs[:, pc_ * 512:(pc_ + 1) * 512],
                                       self.PS[s_][:, 0:512], AF.Exp), "sS")
                        relSc[s_] = ev
                        p.wait("tensor", ev, dvv[kb])
                        p.run("tensor",
                              lambda e, kb=kb, kc=kc, qt=qt, pc_=pc_,
                              first=first: e.matmul(
                                  self.PS[2 + qt][:, 0:512],
                                  v_st[:, kb * T + kc * P:
                                       kb * T + (kc + 1) * P],
                                  probs[:, pc_ * 512:(pc_ + 1) * 512],
                                  start=first, stop=False))
                        relProbs[pc_] = p.inc(
                            "tensor", lambda e, qt=qt, pc_=pc_, first=first:
                            e.matmul(self.PR[qt][0:1, 0:512], self.ones_c[:],
                                     probs[:, pc_ * 512:(pc_ + 1) * 512],
                                     start=first, stop=False), "tS")
                relKhi[kb] = p.val("tS")
                relV[kb] = p.val("tS")
            for qt in range(2):
                for j in range(4):
                    bq = qt * 4 + j
                    s_ = sc % 2
                    sc += 1
                    p.wait("tensor", dow2, lv, relSc[s_])
                    p.run("tensor", lambda e, hb=hb, bq=bq, s_=s_: e.matmul(
                        self.PS[s_][:, 0:P],
                        khi_own[:, hb * T + bq * P:hb * T + (bq + 1) * P],
                        qhi_st[:, hb * T + bq * P:hb * T + (bq + 1) * P],
                        start=True, stop=False))
                    mv = p.inc("tensor", lambda e, hb=hb, bq=bq, s_=s_:
                               e.matmul(
                                   self.PS[s_][:, 0:P],
                                   klo_own[0:65, bq * P:(bq + 1) * P],
                                   qlo_st[0:65, hb * T + bq * P:
                                          hb * T + (bq + 1) * P],
                                   start=False, stop=True), "tS")
                    pd_ = j % 2
                    p.wait("scalar", mv, relProbsD[pd_])
                    ev = p.inc("scalar", lambda e, s_=s_, pd_=pd_:
                               e.activation(probsd[:, pd_ * P:(pd_ + 1) * P],
                                            self.PS[s_][:, 0:P], AF.Exp),
                               "sS")
                    relSc[s_] = ev
                    p.wait("vector", ev, relPdm[pd_])
                    mk = p.inc("vector", lambda e, pd_=pd_: e.tensor_tensor(
                        pdm[:, pd_ * P:(pd_ + 1) * P],
                        probsd[:, pd_ * P:(pd_ + 1) * P], self.trim_b[:],
                        ALU.mult), "vS")
                    relProbsD[pd_] = mk
                    last = (j == 3)
                    p.wait("tensor", mk)
                    p.run("tensor", lambda e, hb=hb, bq=bq, j=j, qt=qt,
                          pd_=pd_, last=last: e.matmul(
                              self.PS[2 + qt][:, j * P:(j + 1) * P],
                              v_own[:, hb * T + bq * P:hb * T + (bq + 1) * P],
                              pdm[:, pd_ * P:(pd_ + 1) * P],
                              start=False, stop=last))
                    relPdm[pd_] = p.inc(
                        "tensor", lambda e, j=j, qt=qt, pd_=pd_, last=last:
                        e.matmul(self.PR[qt][0:1, j * P:(j + 1) * P],
                                 self.ones_c[:],
                                 pdm[:, pd_ * P:(pd_ + 1) * P],
                                 start=False, stop=last), "tS")
                p.wait("vector", p.val("tS"))
                v = p.inc("vector", lambda e, qt=qt: e.reciprocal(
                    rsf[:], self.PR[qt][0:1, 0:512]), "vS")
                relSum[qt] = v
                p.wait("vector", v)
                v = p.inc("vector", lambda e: e.tensor_copy(
                    rhib[0:1, 0:512], rsf[:]), "vS")
                p.wait("vector", v)
                v = p.inc("vector", lambda e: e.tensor_tensor(
                    rhib[0:1, 512:1024], rsf[:], rhib[0:1, 0:512],
                    ALU.subtract), "vS")
                bcb = sc % 2
                sc += 1
                p.wait("tensor", v, relSc[bcb])
                p.run("tensor", lambda e, bcb=bcb: e.matmul(
                    self.PS[bcb][:, 0:512], self.ones_r[:], rhib[0:1, 0:512],
                    start=True, stop=False))
                tv = p.inc("tensor", lambda e, bcb=bcb: e.matmul(
                    self.PS[bcb][:, 0:512], self.ones_r[:],
                    rhib[0:1, 512:1024], start=False, stop=True), "tS")
                p.wait("vector", tv)
                v = p.inc("vector", lambda e, bcb=bcb: e.tensor_copy(
                    bcsb[:], self.PS[bcb][:, 0:512]), "vS")
                relSc[bcb] = v
                p.wait("vector", v)
                relPV[qt] = p.inc(
                    "vector", lambda e, h=h, qt=qt: e.tensor_tensor(
                        self.attn_fm[:, h * T + qt * 512:
                                     h * T + (qt + 1) * 512],
                        self.PS[2 + qt][:, 0:512], bcsb[:], ALU.mult), "vS")
            relQ[hb] = p.val("tS")
            relOwn[hb] = p.val("tS")
        p.barrier()
        esA2.close()

    def wo_phase(self):
        nc, p = self.nc, self.p
        esW = ExitStack()
        woslab = esW.enter_context(
            nc.sbuf_tensor("woslab", [P, 2 * 16 * 512], BF16, side="right"))
        relSlab = [Z, Z]
        relPS = [Z, Z]
        for oc in range(4):
            wb = oc % 2
            p.wait("sync", relSlab[wb])
            dv = p.dma(
                "sync",
                A(woslab, wb * 16 * 512, [2 * 16 * 512, P], [512, 16],
                  [1, 512]),
                A(self.woB, oc * 512, [D, P], [P * D, 16], [1, 512]), "dS")
            for tt in range(NTT):
                pb = (oc * NTT + tt) % 2
                p.wait("tensor", dv, relPS[pb])
                for hh in range(16):
                    fn = (lambda e, hh=hh, tt=tt, oc=oc, pb=pb, wb=wb:
                          e.matmul(
                              self.PS[pb][:, 0:512],
                              self.attn_fm[:, hh * T + tt * P:
                                           hh * T + (tt + 1) * P],
                              woslab[:, wb * 16 * 512 + hh * 512:
                                     wb * 16 * 512 + (hh + 1) * 512],
                              start=(hh == 0), stop=(hh == 15)))
                    if hh == 15:
                        mv = p.inc("tensor", fn, "tS")
                    else:
                        p.run("tensor", fn)
                relSlab[wb] = mv
                xsl = self.xres[:, tt * D + oc * 512:tt * D + (oc + 1) * 512]
                p.wait("vector", mv)
                relPS[pb] = p.inc("vector", lambda e, x=xsl, pb=pb:
                                  e.tensor_tensor(x, x, self.PS[pb][:, 0:512],
                                                  ALU.add), "vS")
        p.barrier()
        esW.close()
        self.esA.close()

    def build(self):
        p = self.p
        self.phase0_and_wcast()
        self.phaseW()
        self.ffn_phase("F1")
        if self.stage >= 2:
            self.kv_q_phases()
        if self.stage >= 3:
            self.attention_phase()
            self.wo_phase()
        if self.stage >= 4:
            self.ffn_phase("F2")
        for tt in range(NTT):
            p.dma("sync", self.out_d[tt * P:(tt + 1) * P, :],
                  self.xres[:, tt * D:(tt + 1) * D], "dS")
        p.wait("sync", p.val("dS"))
        with self.nc.Block() as block:
            block.sync(lambda e: [fn(e) for fn in p.tapes["sync"]])
            block.gpsimd(lambda e: [fn(e) for fn in p.tapes["gpsimd"]])
            block.tensor(lambda e: [fn(e) for fn in p.tapes["tensor"]])
            block.vector(lambda e: [fn(e) for fn in p.tapes["vector"]])
            block.scalar(lambda e: [fn(e) for fn in p.tapes["scalar"]])
        self.es.close()
        return self.nc


def host_inputs(hs, wqa, gqa, wqb, wkva, gkva, wkvb, wo, wffn, bffn):
    hs = np.ascontiguousarray(np.asarray(hs, dtype=np.float32)).reshape(
        NC_ * T, D)
    e = np.arange(0, ROPE, 2, dtype=np.float32) / np.float32(ROPE)
    inv = (1.0 / (10000.0 ** e.astype(np.float64))).astype(np.float32)
    qflag = np.zeros((NTT, T), np.float32)
    for j in range(NTT):
        qflag[j, j * P:(j + 1) * P] = -30000.0
    trim = (np.arange(P)[:, None] <= np.arange(P)[None, :]).astype(np.float32)
    iden = np.eye(P, dtype=np.float32)
    gqa_c = np.ascontiguousarray(
        np.asarray(gqa, np.float32).reshape(QR // P, P).T)
    gkva_c = np.ascontiguousarray(
        np.asarray(gkva, np.float32).reshape(KVR // P, P).T)
    brep = np.ascontiguousarray(
        np.broadcast_to(np.asarray(bffn, np.float32)[None, :], (P, D)))
    com = dict(
        wqa=np.ascontiguousarray(np.asarray(wqa, np.float32)),
        gqa=gqa_c,
        wqb=np.ascontiguousarray(np.asarray(wqb, np.float32)),
        wkva=np.ascontiguousarray(np.asarray(wkva, np.float32)),
        gkva=gkva_c,
        wkvb=np.ascontiguousarray(np.asarray(wkvb, np.float32)),
        wo=np.ascontiguousarray(np.asarray(wo, np.float32)),
        wffn=np.ascontiguousarray(np.asarray(wffn, np.float32)),
        brep=brep, qflag=qflag, trim=trim, iden=iden)
    maps = []
    for c in range(NC_):
        pos = np.arange(c * T, (c + 1) * T, dtype=np.float32)
        ang = (pos[:, None] * inv[None, :]).astype(np.float32)
        ang = np.concatenate([ang, ang], axis=1)
        cosw = np.cos(ang.astype(np.float64)).astype(np.float32)
        snw = np.sin(ang.astype(np.float64)).astype(np.float32)
        snw[:, 0:32] = -snw[:, 0:32]
        kr = np.zeros((NTT, NC_ * T), np.float32)
        for j in range(NTT):
            for r in range(NC_):
                for kc in range(NTT):
                    if 8 * r + kc >= 8 * c + j:
                        kr[j, r * T + kc * P:r * T + (kc + 1) * P] = 1.0
        m = dict(com)
        m["x0"] = np.ascontiguousarray(hs[c * T:(c + 1) * T])
        m["cosw"] = cosw
        m["snw"] = np.ascontiguousarray(snw)
        m["krows"] = kr
        maps.append(m)
    return maps


def kernel(L0_attn_hidden_states, L0_attn_Wqa, L0_attn_g_qa, L0_attn_Wqb,
           L0_attn_Wkva, L0_attn_g_kva, L0_attn_Wkvb, L0_attn_Wo,
           L0_ffn_weight, L0_ffn_bias):
    maps = host_inputs(L0_attn_hidden_states, L0_attn_Wqa, L0_attn_g_qa,
                       L0_attn_Wqb, L0_attn_Wkva, L0_attn_g_kva,
                       L0_attn_Wkvb, L0_attn_Wo, L0_ffn_weight, L0_ffn_bias)
    nc = MK(STAGE).build()
    res = run_bass_kernel_spmd(nc, maps, core_ids=list(range(NC_)))
    out = np.concatenate([np.asarray(r["out"]) for r in res.results], axis=0)
    return np.ascontiguousarray(out).reshape(1, NC_ * T, D).astype(np.float32)


# version 04
# speedup vs baseline: 1.6821x; 1.6821x over previous
import numpy as np
import ml_dtypes
import concourse.bass as bass
import concourse.mybir as mybir
from concourse.bass_utils import run_bass_kernel_spmd
from contextlib import ExitStack

F32 = mybir.dt.float32
BF16 = mybir.dt.bfloat16
F16 = mybir.dt.float16
AF = mybir.ActivationFunctionType
ALU = mybir.AluOpType
AX = mybir.AxisListType

NC_ = 8
P = 128
T = 1024
NTT = 8
D = 2048
H = 16
QR, KVR, ROPE, NOPE, VD = 768, 512, 64, 128, 128
QKD = NOPE + ROPE
SCALE = float(QKD) ** -0.5
MAGIC = 1536.0
STAGE = 4

KHI_OFF = 0
KLO_OFF = KHI_OFF + H * P * T
V_OFF = KLO_OFF + ROPE * T
KN_OFF = V_OFF + H * T * VD
SZKV = KN_OFF + 128

ENGS = ("sync", "gpsimd", "tensor", "vector", "scalar")
Z = ("vS", 0)


def A(t, off, *dims):
    return bass.AP(t, off, [list(d) for d in dims])


class Plan:
    def __init__(self, nc, es):
        self.nc = nc
        self.tapes = {e: [] for e in ENGS}
        self.sems = {}
        self.last_wait = {}
        for name in ("dS", "dG", "cc", "tS", "vS", "sS",
                     "bSY", "bGP", "bT", "bV", "bS"):
            h = es.enter_context(nc.semaphore(name))
            self.sems[name] = [h, 0]

    def val(self, sem):
        return (sem, self.sems[sem][1])

    def run(self, eng, fn):
        self.tapes[eng].append(fn)

    def inc(self, eng, fn, sem, k=1):
        rec = self.sems[sem]
        rec[1] += k
        h, v = rec[0], rec[1]
        self.tapes[eng].append(lambda e, fn=fn, h=h, k=k: fn(e).then_inc(h, k))
        return (sem, v)

    def wait(self, eng, *rels):
        best = {}
        for rel in rels:
            if rel is None:
                continue
            s, v = rel
            if v > best.get(s, 0):
                best[s] = v
        for s, v in best.items():
            key = (eng, s)
            if self.last_wait.get(key, -1) >= v:
                continue
            self.last_wait[key] = v
            h = self.sems[s][0]
            self.tapes[eng].append(lambda e, h=h, v=v: e.wait_ge(h, v))

    def dma(self, eng, out_ap, in_ap, sem):
        return self.inc(eng, lambda e, o=out_ap, i=in_ap:
                        e.dma_start(out=o, in_=i), sem, 16)

    def barrier(self):
        e2s = {"sync": "bSY", "gpsimd": "bGP", "tensor": "bT",
               "vector": "bV", "scalar": "bS"}
        self.wait("sync", self.val("dS"))
        self.wait("gpsimd", self.val("dG"), self.val("cc"))
        for e, s in e2s.items():
            rec = self.sems[s]
            rec[1] += 1
            h = rec[0]
            self.tapes[e].append(lambda x, h=h: x.sem_inc(h, 1))
        for e in ENGS:
            self.wait(e, *[self.val(s) for e2, s in e2s.items() if e2 != e])


class MK:
    def __init__(self, stage=STAGE):
        self.stage = stage
        nc = bass.Bass("TRN2", target_bir_lowering=False,
                       detect_race_conditions=False)
        self.nc = nc
        self.es = ExitStack()
        self.p = Plan(nc, self.es)
        dr = nc.dram_tensor
        self.x0_d = dr("x0", [T, D], F32, kind="ExternalInput")
        self.wqa_d = dr("wqa", [D, QR], F32, kind="ExternalInput")
        self.gqa_d = dr("gqa", [P, QR // P], F32, kind="ExternalInput")
        self.wqb_d = dr("wqb", [QR, H * QKD], F32, kind="ExternalInput")
        self.wkva_d = dr("wkva", [D, KVR + ROPE], F32, kind="ExternalInput")
        self.gkva_d = dr("gkva", [P, KVR // P], F32, kind="ExternalInput")
        self.wkvb_d = dr("wkvb", [KVR, H * (NOPE + VD)], F32, kind="ExternalInput")
        self.wo_d = dr("wo", [H * VD, D], F32, kind="ExternalInput")
        self.wffn_d = dr("wffn", [D, D], F32, kind="ExternalInput")
        self.brep_d = dr("brep", [P, D], F32, kind="ExternalInput")
        self.cos_d = dr("cosw", [T, ROPE], F32, kind="ExternalInput")
        self.sn_d = dr("snw", [T, ROPE], F32, kind="ExternalInput")
        self.qflag_d = dr("qflag", [NTT, T], F32, kind="ExternalInput")
        self.krows_d = dr("krows", [NTT, NC_ * T], F32, kind="ExternalInput")
        self.trim_d = dr("trim", [P, P], F32, kind="ExternalInput")
        self.iden_d = dr("iden", [P, P], F32, kind="ExternalInput")
        self.out_d = dr("out", [T, D], F32, kind="ExternalOutput")
        self.kvown = dr("kvown", [SZKV], BF16)
        self.kvall = dr("kvall", [NC_ * SZKV], BF16)
        self.weffT_d = dr("weffT", [D, D], BF16)
        self.wqaB = dr("wqaB", [D, QR], BF16)
        self.wqbB = dr("wqbB", [QR, H * QKD], BF16)
        self.wkvaB = dr("wkvaB", [D, KVR + ROPE], BF16)
        self.wkvbB = dr("wkvbB", [KVR, H * (NOPE + VD)], BF16)
        self.woB = dr("woB", [H * VD, D], BF16)
        self.wsrowD = dr("wsrowD", [1, D], F32)
        self.qhi_dd = dr("qhid", [H * NOPE * T], BF16)
        self.qlo_dd = dr("qlod", [H * 65 * T], BF16)

        L = lambda nm, shp, dt: self.es.enter_context(
            nc.sbuf_tensor(nm, shp, dt, side="left"))
        self.xres = L("xres", [P, NTT * D], F32)
        self.klo_all = L("klo_all", [73, NC_ * T], BF16)
        self.trim_b = L("trim_b", [P, P], BF16)
        self.iden_b = L("iden_b", [P, P], BF16)
        self.iden_f = L("iden_f", [P, P], F32)
        self.ones_c = L("ones_c", [P, 1], BF16)
        self.ones_r = L("ones_r", [1, P], BF16)
        self.stats = L("stats", [P, 64], F32)
        self.xs_all = L("xs_all", [P, NTT], F32)
        self.kmaxbc = L("kmaxbc", [P, H], F32)
        self.knrow_b = L("knrow_b", [1, P], BF16)
        self.gqs_c = L("gqs_c", [P, QR // P], F32)
        self.gkv_c = L("gkv_c", [P, KVR // P], F32)

        self.PS = [self.es.enter_context(
            nc.psum_tensor(f"ps{i}", [P, 512], F32)) for i in range(4)]
        self.PR = [self.es.enter_context(
            nc.psum_tensor(f"pr{i}", [1, 512], F32)) for i in range(2)]
        self.PTx = [self.es.enter_context(
            nc.psum_tensor(f"pt{i}", [P, P], BF16)) for i in range(2)]

    def cast_to_dram(self, src, rows, cols, dst, stg, bnc, scale_col=None):
        p = self.p
        piece = 1024
        if not hasattr(self, "_castrel"):
            self._castrel = ([Z, Z], [Z, Z], [0])
        relS, relB, ii = self._castrel
        i = ii[0]
        for c in range(rows // P):
            for off in range(0, cols, piece):
                w = min(piece, cols - off)
                b = i % 2
                i += 1
                p.wait("sync", relS[b])
                dv = p.dma("sync", stg[:, b * 1024:b * 1024 + w],
                           src[c * P:(c + 1) * P, off:off + w], "dS")
                p.wait("vector", dv, relB[b])
                sap = stg[:, b * 1024:b * 1024 + w]
                bap = bnc[:, b * 1024:b * 1024 + w]
                if scale_col is None:
                    relS[b] = p.inc("vector", lambda e, o=bap, s=sap:
                                    e.tensor_copy(o, s), "vS")
                else:
                    sc = scale_col[:, c:c + 1]
                    relS[b] = p.inc("vector", lambda e, o=bap, s=sap, sc=sc:
                                    e.tensor_single_scalar(o, s, sc, ALU.mult),
                                    "vS")
                p.wait("gpsimd", relS[b])
                relB[b] = p.dma("gpsimd",
                                dst[c * P:(c + 1) * P, off:off + w], bap, "dG")
        ii[0] = i

    def phase0_and_wcast(self):
        nc, p = self.nc, self.p
        es0 = ExitStack()
        R = lambda nm, shp, dt: es0.enter_context(
            nc.sbuf_tensor(nm, shp, dt, side="right"))
        stg = R("stg0", [P, 2 * 1024], F32)
        bnc = R("bnc0", [P, 2 * 1024], BF16)
        k8 = R("k8", [NTT, T], BF16)
        for t_, v_ in ((self.ones_c, 1.0), (self.ones_r, 1.0)):
            p.run("vector", lambda e, t=t_, v=v_: e.memset(t[:], v))
        p.run("vector", lambda e: e.memset(self.klo_all[64:65, :], -1.0))
        for tt in range(NTT):
            p.dma("sync", self.xres[:, tt * D:(tt + 1) * D],
                  self.x0_d[tt * P:(tt + 1) * P, :], "dS")
        p.dma("sync", stg[:, 0:128], self.trim_d[:, :], "dS")
        p.dma("sync", stg[:, 128:256], self.iden_d[:, :], "dS")
        p.dma("sync", stg[:, 256:256 + QR // P], self.gqa_d[:, :], "dS")
        dv = p.dma("sync", stg[:, 280:280 + KVR // P], self.gkva_d[:, :], "dS")
        p.wait("vector", dv)
        p.inc("vector", lambda e: e.tensor_copy(self.trim_b[:], stg[:, 0:128]), "vS")
        p.inc("vector", lambda e: e.tensor_copy(self.iden_b[:], stg[:, 128:256]), "vS")
        p.inc("vector", lambda e: e.tensor_copy(self.iden_f[:], stg[:, 128:256]), "vS")
        p.inc("vector", lambda e: e.tensor_single_scalar(
            self.gqs_c[:], stg[:, 256:256 + QR // P], SCALE, ALU.mult), "vS")
        gv = p.inc("vector", lambda e: e.tensor_copy(
            self.gkv_c[:], stg[:, 280:280 + KVR // P]), "vS")
        for r in range(NC_):
            p.wait("sync", p.val("vS"), p.val("dG"))
            dvr = p.dma("sync", stg[0:NTT, 0:T],
                        self.krows_d[:, r * T:(r + 1) * T], "dS")
            p.wait("vector", dvr)
            cv = p.inc("vector", lambda e: e.tensor_copy(
                k8[:], stg[0:NTT, 0:T]), "vS")
            p.wait("gpsimd", cv)
            p.dma("gpsimd", self.klo_all[65:73, r * T:(r + 1) * T],
                  k8[:], "dG")
        p.wait("vector", gv)
        p.wait("sync", p.val("vS"))
        self.cast_to_dram(self.wqa_d, D, QR, self.wqaB, stg, bnc)
        self.cast_to_dram(self.wqb_d, QR, H * QKD, self.wqbB, stg, bnc,
                          scale_col=self.gqs_c)
        self.cast_to_dram(self.wkva_d, D, KVR + ROPE, self.wkvaB, stg, bnc)
        self.cast_to_dram(self.wkvb_d, KVR, H * (NOPE + VD), self.wkvbB,
                          stg, bnc, scale_col=self.gkv_c)
        self.cast_to_dram(self.wo_d, H * VD, D, self.woB, stg, bnc)
        p.barrier()
        es0.close()

    def phaseW(self):
        nc, p, st = self.nc, self.p, self.stats
        esW = ExitStack()
        R = lambda nm, shp, dt: esW.enter_context(
            nc.sbuf_tensor(nm, shp, dt, side="right"))
        wstage = R("wstage", [P, 2 * D], F32)
        q16b = R("q16b", [P, D], F16)
        weffb = R("weffb", [P, D], BF16)
        wTst = R("wTst", [P, 4 * P], BF16)
        wsrow = R("wsrow", [1, D], F32)
        relW = [Z, Z]
        relPT = [Z, Z]
        relWT = [Z] * 4
        relWS = Z
        for r in range(D // P):
            buf = r % 2
            p.wait("sync", relW[buf])
            dv = p.dma("sync", wstage[:, buf * D:(buf + 1) * D],
                       self.wffn_d[r * P:(r + 1) * P, :], "dS")
            p.wait("vector", dv)
            w = wstage[:, buf * D:(buf + 1) * D]
            v = p.inc("vector", lambda e, w=w: e.reduce_max(
                st[:, 0:1], w, axis=AX.X, apply_absolute_value=True), "vS")
            p.wait("vector", v)
            v = p.inc("vector", lambda e: e.tensor_scalar(
                st[:, 1:2], st[:, 0:1], 1e-8, 1.0 / 127.0,
                ALU.max, ALU.mult), "vS")
            p.wait("vector", v)
            v = p.inc("vector", lambda e: e.reciprocal(
                st[:, 3:4], st[:, 1:2]), "vS")
            p.wait("vector", v)
            p.inc("vector", lambda e, w=w: e.tensor_scalar(
                q16b[:], w, st[:, 3:4], MAGIC, ALU.mult, ALU.add), "vS")
            relW[buf] = p.inc("vector", lambda e: e.tensor_scalar(
                weffb[:], q16b[:], -MAGIC, 0.5, ALU.add, ALU.mult), "vS")
            wv = p.inc("vector", lambda e: e.tensor_copy(
                st[:, 4:5], st[:, 1:2]), "vS")
            p.wait("tensor", wv, relWS)
            tws = p.inc("tensor", lambda e: e.transpose(
                self.PS[2][0:1, 0:P], st[:, 4:5], self.iden_f[:]), "tS")
            p.wait("vector", tws)
            relWS = p.inc("vector", lambda e, r=r: e.tensor_copy(
                wsrow[0:1, r * P:(r + 1) * P], self.PS[2][0:1, 0:P]), "vS")
            for c in range(D // P):
                sl = c % 2
                p.wait("tensor", relW[buf], relPT[sl])
                tv = p.inc("tensor", lambda e, c=c, sl=sl: e.transpose(
                    self.PTx[sl][:, 0:P],
                    weffb[:, c * P:(c + 1) * P], self.iden_b[:]), "tS")
                sb = c % 4
                p.wait("vector", tv, relWT[sb])
                relPT[sl] = p.inc("vector", lambda e, sb=sb, sl=sl:
                                  e.tensor_copy(
                                      wTst[:, sb * P:(sb + 1) * P],
                                      self.PTx[sl][:, 0:P]), "vS")
                p.wait("gpsimd", relPT[sl])
                relWT[sb] = p.dma(
                    "gpsimd",
                    self.weffT_d[c * P:(c + 1) * P, r * P:(r + 1) * P],
                    wTst[:, sb * P:(sb + 1) * P], "dG")
        p.wait("sync", relWS)
        p.dma("sync", self.wsrowD[:, :], wsrow[:], "dS")
        p.barrier()
        esW.close()

    def ln_stats(self, src_fn, scratch_fn, ncols, eps):
        p, st = self.p, self.stats
        sv = None
        for tt in range(NTT):
            x = src_fn(tt)
            p.inc("vector", lambda e, x=x, tt=tt: e.reduce_sum(
                st[:, tt:tt + 1], x, axis=AX.X), "vS")
            sc = scratch_fn(tt)
            sv = p.inc("scalar", lambda e, x=x, tt=tt, sc=sc: e.activation(
                sc, x, AF.Square, accum_out=st[:, 8 + tt:9 + tt]), "sS")
        p.wait("vector", sv, p.val("vS"))
        v = p.inc("vector", lambda e: e.tensor_single_scalar(
            st[:, 16:24], st[:, 0:8], 1.0 / ncols, ALU.mult), "vS")
        p.wait("vector", v)
        v = p.inc("vector", lambda e: e.tensor_tensor(
            st[:, 24:32], st[:, 16:24], st[:, 16:24], ALU.mult), "vS")
        p.wait("vector", v)
        v = p.inc("vector", lambda e, n=ncols: e.scalar_tensor_tensor(
            st[:, 32:40], st[:, 8:16], 1.0 / n, st[:, 24:32],
            ALU.mult, ALU.subtract), "vS")
        p.wait("scalar", v)
        sv = p.inc("scalar", lambda e, eps=eps: e.activation(
            st[:, 40:48], st[:, 32:40], AF.Sqrt, bias=np.float32(eps)), "sS")
        p.wait("vector", sv)
        v = p.inc("vector", lambda e: e.reciprocal(
            st[:, 48:56], st[:, 40:48]), "vS")
        p.wait("vector", v)
        v = p.inc("vector", lambda e: e.scalar_tensor_tensor(
            st[:, 56:64], st[:, 16:24], -1.0, st[:, 48:56],
            ALU.mult, ALU.mult), "vS")
        p.wait("scalar", v)
        return v

    def ffn_phase(self, tag):
        nc, p, st = self.nc, self.p, self.stats
        esF = ExitStack()
        R = lambda nm, shp, dt: esF.enter_context(
            nc.sbuf_tensor(nm + tag, shp, dt, side="right"))
        xqT = R("xqT", [P, 16 * T], BF16)
        bias05 = R("bias05", [P, D], F32)
        wsb_bc = R("wsb_bc", [P, D], F32)
        esFa = ExitStack()
        Ra = lambda nm, shp, dt: esFa.enter_context(
            nc.sbuf_tensor(nm + tag, shp, dt, side="right"))
        lnf = Ra("lnf", [P, 2 * D], F32)
        q16 = Ra("q16f", [P, D], F16)
        xqb = Ra("xqb", [P, 2 * D], BF16)
        wrtmp = Ra("wrtmp", [1, D], F32)
        wrb = Ra("wrb", [1, 2 * D], BF16)

        for q in range(4):
            p.wait("sync", p.val("vS"))
            dvq = p.dma("sync", lnf[:, 0:512],
                        self.brep_d[:, q * 512:(q + 1) * 512], "dS")
            p.wait("vector", dvq)
            p.inc("vector", lambda e, q=q: e.tensor_single_scalar(
                bias05[:, q * 512:(q + 1) * 512], lnf[:, 0:512],
                0.5, ALU.mult), "vS")
        p.wait("sync", p.val("vS"))
        dvw = p.dma("sync", wrtmp[:, :], self.wsrowD[:, :], "dS")
        p.wait("vector", dvw)
        v = p.inc("vector", lambda e: e.tensor_copy(wrb[0:1, 0:D], wrtmp[:]),
                  "vS")
        p.wait("vector", v)
        v = p.inc("vector", lambda e: e.tensor_tensor(
            wrb[0:1, D:2 * D], wrtmp[:], wrb[0:1, 0:D], ALU.subtract), "vS")
        relBC = Z
        for oc in range(4):
            p.wait("tensor", v, relBC)
            p.run("tensor", lambda e, oc=oc: e.matmul(
                self.PS[2][:, 0:512], self.ones_r[:],
                wrb[0:1, oc * 512:(oc + 1) * 512], start=True, stop=False))
            tv = p.inc("tensor", lambda e, oc=oc: e.matmul(
                self.PS[2][:, 0:512], self.ones_r[:],
                wrb[0:1, D + oc * 512:D + (oc + 1) * 512],
                start=False, stop=True), "tS")
            p.wait("vector", tv)
            relBC = p.inc("vector", lambda e, oc=oc: e.tensor_copy(
                wsb_bc[:, oc * 512:(oc + 1) * 512], self.PS[2][:, 0:512]),
                "vS")
        p.wait("scalar", p.val("vS"))

        self.ln_stats(lambda tt: self.xres[:, tt * D:(tt + 1) * D],
                      lambda tt: lnf[:, (tt % 2) * D:(tt % 2 + 1) * D],
                      D, 1e-5)
        relLnV = [Z, Z]
        relPT = [Z, Z]
        relXqT = [Z, Z]
        for tt in range(NTT):
            sb = tt % 2
            x = self.xres[:, tt * D:(tt + 1) * D]
            ln = lnf[:, sb * D:(sb + 1) * D]
            p.wait("scalar", relLnV[sb])
            lv = p.inc("scalar", lambda e, x=x, tt=tt, ln=ln: e.activation(
                ln, x, AF.Identity, scale=st[:, 48 + tt:49 + tt],
                bias=st[:, 56 + tt:57 + tt]), "sS")
            p.wait("vector", lv)
            v = p.inc("vector", lambda e, ln=ln, tt=tt: e.reduce_max(
                st[:, tt:tt + 1], ln, axis=AX.X, apply_absolute_value=True),
                "vS")
            p.wait("vector", v)
            v = p.inc("vector", lambda e, tt=tt: e.tensor_scalar(
                self.xs_all[:, tt:tt + 1], st[:, tt:tt + 1], 1.0 / 127.0,
                1e-12, ALU.mult, ALU.max), "vS")
            p.wait("vector", v)
            v = p.inc("vector", lambda e, tt=tt: e.reciprocal(
                st[:, 8 + tt:9 + tt], self.xs_all[:, tt:tt + 1]), "vS")
            p.wait("vector", v)
            relLnV[sb] = p.inc("vector", lambda e, ln=ln, tt=tt:
                               e.tensor_scalar(q16[:], ln,
                                               st[:, 8 + tt:9 + tt], MAGIC,
                                               ALU.mult, ALU.add), "vS")
            p.wait("vector", relXqT[sb])
            xq = p.inc("vector", lambda e, sb=sb: e.tensor_single_scalar(
                xqb[:, sb * D:(sb + 1) * D], q16[:], -MAGIC, ALU.add), "vS")
            for c in range(D // P):
                sl = c % 2
                p.wait("tensor", xq, relPT[sl])
                tv = p.inc("tensor", lambda e, c=c, sl=sl, sb=sb: e.transpose(
                    self.PTx[sl][:, 0:P],
                    xqb[:, sb * D + c * P:sb * D + (c + 1) * P],
                    self.iden_b[:]), "tS")
                if c == D // P - 1:
                    relXqT[sb] = tv
                p.wait("vector", tv)
                relPT[sl] = p.inc(
                    "vector", lambda e, c=c, tt=tt, sl=sl: e.tensor_copy(
                        xqT[:, c * T + tt * P:c * T + (tt + 1) * P],
                        self.PTx[sl][:, 0:P]), "vS")
        xqT_rdy = p.val("vS")
        esFa.close()
        wslab = R("wslab", [P, 2 * 16 * 512], BF16)
        usb = R("usb", [P, 2 * 512], F32)

        relSlab = [Z, Z]
        relPS = [Z, Z]
        relU = [Z, Z]
        for oc in range(4):
            wb = oc % 2
            p.wait("sync", relSlab[wb], xqT_rdy)
            dv = p.dma(
                "sync",
                A(wslab, wb * 16 * 512, [2 * 16 * 512, P], [512, 16], [1, 512]),
                A(self.weffT_d, oc * 512, [D, P], [P * D, 16], [1, 512]),
                "dS")
            for tt in range(NTT):
                pb = (oc * NTT + tt) % 2
                p.wait("tensor", dv, xqT_rdy, relPS[pb])
                for c in range(16):
                    fn = (lambda e, c=c, tt=tt, wb=wb, pb=pb:
                          e.matmul(self.PS[pb][:, 0:512],
                                   xqT[:, c * T + tt * P:c * T + (tt + 1) * P],
                                   wslab[:, wb * 16 * 512 + c * 512:
                                         wb * 16 * 512 + (c + 1) * 512],
                                   start=(c == 0), stop=(c == 15)))
                    if c == 15:
                        mv = p.inc("tensor", fn, "tS")
                    else:
                        p.run("tensor", fn)
                relSlab[wb] = mv
                p.wait("vector", mv, relU[pb])
                p.inc("vector", lambda e, pb=pb, tt=tt, oc=oc:
                      e.scalar_tensor_tensor(
                          usb[:, pb * 512:(pb + 1) * 512],
                          self.PS[pb][:, 0:512], self.xs_all[:, tt:tt + 1],
                          wsb_bc[:, oc * 512:(oc + 1) * 512],
                          ALU.mult, ALU.mult), "vS")
                relPS[pb] = p.val("vS")
                p.inc("vector", lambda e, pb=pb, oc=oc: e.tensor_tensor(
                    usb[:, pb * 512:(pb + 1) * 512],
                    usb[:, pb * 512:(pb + 1) * 512],
                    bias05[:, oc * 512:(oc + 1) * 512], ALU.add), "vS")
                xsl = self.xres[:, tt * D + oc * 512:tt * D + (oc + 1) * 512]
                relU[pb] = p.inc(
                    "vector", lambda e, x=xsl, pb=pb: e.tensor_tensor(
                        x, x, usb[:, pb * 512:(pb + 1) * 512], ALU.add), "vS")
        p.barrier()
        esF.close()

    def kv_q_phases(self):
        import os
        CUT = os.environ.get("KVQ_CUT", "")
        nc, p, st = self.nc, self.p, self.stats
        esM = ExitStack()
        R = lambda nm, shp, dt: esM.enter_context(
            nc.sbuf_tensor(nm, shp, dt, side="right"))
        qcsT = R("qcsT", [P, 6 * T], BF16)
        cosb = R("cosb", [P, NTT * ROPE], F32)
        snb = R("snb", [P, NTT * ROPE], F32)
        lnscr = R("lnscr", [P, D], F32)
        esL = ExitStack()
        ln2T = esL.enter_context(
            nc.sbuf_tensor("ln2T", [P, 16 * T], BF16, side="right"))
        ln2b = esL.enter_context(
            nc.sbuf_tensor("ln2b", [P, 2 * D], BF16, side="right"))
        self.ln2T_h = ln2T
        p.dma("sync",
              A(cosb, 0, [NTT * ROPE, P], [ROPE, NTT], [1, ROPE]),
              A(self.cos_d, 0, [ROPE, P], [P * ROPE, NTT], [1, ROPE]), "dS")
        cosrdy = p.dma("sync",
                       A(snb, 0, [NTT * ROPE, P], [ROPE, NTT], [1, ROPE]),
                       A(self.sn_d, 0, [ROPE, P], [P * ROPE, NTT], [1, ROPE]),
                       "dS")

        self.ln_stats(lambda tt: self.xres[:, tt * D:(tt + 1) * D],
                      lambda tt: lnscr[:, 0:D], D, 1e-5)
        relLn2 = [Z, Z]
        relPT = [Z, Z]
        for tt in range(NTT):
            sb = tt % 2
            x = self.xres[:, tt * D:(tt + 1) * D]
            ln = ln2b[:, sb * D:(sb + 1) * D]
            p.wait("scalar", relLn2[sb])
            lv = p.inc("scalar", lambda e, x=x, tt=tt, ln=ln: e.activation(
                ln, x, AF.Identity, scale=st[:, 48 + tt:49 + tt],
                bias=st[:, 56 + tt:57 + tt]), "sS")
            for c in range(D // P):
                sl = c % 2
                p.wait("tensor", lv, relPT[sl])
                tv = p.inc("tensor", lambda e, c=c, sl=sl, ln=ln: e.transpose(
                    self.PTx[sl][:, 0:P],
                    ln[:, c * P:(c + 1) * P], self.iden_b[:]), "tS")
                if c == D // P - 1:
                    relLn2[sb] = tv
                p.wait("vector", tv)
                relPT[sl] = p.inc(
                    "vector", lambda e, c=c, tt=tt, sl=sl: e.tensor_copy(
                        ln2T[:, c * T + tt * P:c * T + (tt + 1) * P],
                        self.PTx[sl][:, 0:P]), "vS")
        ln2T_rdy = p.val("vS")
        if CUT == "ln2":
            p.barrier()
            esL.close()
            esM.close()
            return

        esK = ExitStack()
        RK = lambda nm, shp, dt: esK.enter_context(
            nc.sbuf_tensor(nm, shp, dt, side="right"))
        wkva_sb = RK("wkva_sb", [P, 16 * (KVR + ROPE)], BF16)
        kvbsl = RK("kvbsl", [P, 2 * 4 * 512], BF16)
        kvc = RK("kvc", [P, 2 * KVR], F32)
        rkv = RK("rkv", [P, 2 * KVR], BF16)
        rkvT = RK("rkvT", [P, 4 * T], BF16)
        kvexp = RK("kvexp", [P, 2 * 4096], BF16)
        krall = RK("krall", [P, NTT * ROPE], BF16)
        ropet = RK("ropet", [P, 2 * ROPE], F32)
        kTst = RK("kTst", [P, 4 * P], BF16)
        knall = RK("knall", [P, P], F32)
        krn = RK("krn", [P, NTT], F32)
        knsc = RK("knsc", [P, KVR], BF16)

        wkva_rdy = p.dma(
            "sync",
            A(wkva_sb, 0, [16 * (KVR + ROPE), P], [KVR + ROPE, 16],
              [1, KVR + ROPE]),
            A(self.wkvaB, 0, [KVR + ROPE, P], [P * (KVR + ROPE), 16],
              [1, KVR + ROPE]), "dS")

        relPSa = [Z, Z]
        relPSb = [Z, Z]
        relKvcS = [Z, Z]
        relRkvT = [Z, Z]
        relPT = [Z, Z]
        for tt in range(NTT):
            sb = tt % 2
            pa, pb_ = sb, 2 + sb
            p.wait("tensor", ln2T_rdy, wkva_rdy, relPSa[sb], relPSb[sb])
            for c in range(16):
                fa = (lambda e, c=c, tt=tt, pa=pa: e.matmul(
                    self.PS[pa][:, 0:512],
                    ln2T[:, c * T + tt * P:c * T + (tt + 1) * P],
                    wkva_sb[:, c * (KVR + ROPE):c * (KVR + ROPE) + KVR],
                    start=(c == 0), stop=(c == 15)))
                fb = (lambda e, c=c, tt=tt, pb_=pb_: e.matmul(
                    self.PS[pb_][:, 0:ROPE],
                    ln2T[:, c * T + tt * P:c * T + (tt + 1) * P],
                    wkva_sb[:, c * (KVR + ROPE) + KVR:(c + 1) * (KVR + ROPE)],
                    start=(c == 0), stop=(c == 15)))
                if c == 15:
                    mva = p.inc("tensor", fa, "tS")
                    mvb = p.inc("tensor", fb, "tS")
                else:
                    p.run("tensor", fa)
                    p.run("tensor", fb)
            p.wait("vector", mva, relKvcS[sb])
            cv = p.inc("vector", lambda e, pa=pa, sb=sb: e.tensor_copy(
                kvc[:, sb * KVR:(sb + 1) * KVR], self.PS[pa][:, 0:512]), "vS")
            relPSa[sb] = cv
            co = cosb[:, tt * ROPE:(tt + 1) * ROPE]
            sn = snb[:, tt * ROPE:(tt + 1) * ROPE]
            rt, rs = ropet[:, 0:ROPE], ropet[:, ROPE:2 * ROPE]
            p.wait("vector", mvb, cosrdy)
            p.inc("vector", lambda e, pb_=pb_, co=co, rt=rt: e.tensor_tensor(
                rt, self.PS[pb_][:, 0:ROPE], co, ALU.mult), "vS")
            p.inc("vector", lambda e, pb_=pb_, sn=sn, rs=rs: e.tensor_tensor(
                rs[:, 0:32], self.PS[pb_][:, 32:64], sn[:, 0:32], ALU.mult),
                "vS")
            v = p.inc("vector", lambda e, pb_=pb_, sn=sn, rs=rs:
                      e.tensor_tensor(rs[:, 32:64], self.PS[pb_][:, 0:32],
                                      sn[:, 32:64], ALU.mult), "vS")
            relPSb[sb] = v
            p.wait("vector", v)
            krv = p.inc("vector", lambda e, tt=tt, rt=rt, rs=rs:
                        e.tensor_tensor(krall[:, tt * ROPE:(tt + 1) * ROPE],
                                        rt, rs, ALU.add), "vS")
            p.wait("scalar", krv)
            p.inc("scalar", lambda e, tt=tt: e.activation(
                knsc[:, 0:ROPE], krall[:, tt * ROPE:(tt + 1) * ROPE],
                AF.Square, accum_out=krn[:, tt:tt + 1]), "sS")
            p.wait("scalar", cv)
            sv = p.inc("scalar", lambda e, sb=sb: e.activation(
                lnscr[:, 0:KVR], kvc[:, sb * KVR:(sb + 1) * KVR],
                AF.Square, accum_out=st[:, 32:33]), "sS")
            p.wait("scalar", sv)
            sv = p.inc("scalar", lambda e: e.activation(
                st[:, 40:41], st[:, 32:33], AF.Sqrt,
                scale=np.float32(1.0 / KVR), bias=np.float32(1e-6)), "sS")
            p.wait("vector", sv)
            v = p.inc("vector", lambda e: e.reciprocal(
                st[:, 48:49], st[:, 40:41]), "vS")
            p.wait("scalar", v, relRkvT[sb])
            rv = p.inc("scalar", lambda e, sb=sb: e.activation(
                rkv[:, sb * KVR:(sb + 1) * KVR],
                kvc[:, sb * KVR:(sb + 1) * KVR],
                AF.Identity, scale=st[:, 48:49]), "sS")
            relKvcS[sb] = rv
            for c in range(4):
                sl = c % 2
                p.wait("tensor", rv, relPT[sl])
                tv = p.inc("tensor", lambda e, c=c, sl=sl, sb=sb: e.transpose(
                    self.PTx[sl][:, 0:P],
                    rkv[:, sb * KVR + c * P:sb * KVR + (c + 1) * P],
                    self.iden_b[:]), "tS")
                if c == 3:
                    relRkvT[sb] = tv
                p.wait("vector", tv)
                relPT[sl] = p.inc(
                    "vector", lambda e, c=c, tt=tt, sl=sl: e.tensor_copy(
                        rkvT[:, c * T + tt * P:c * T + (tt + 1) * P],
                        self.PTx[sl][:, 0:P]), "vS")
        rkvT_rdy = p.val("vS")
        if CUT == "kv":
            p.barrier()
            esK.close()
            esL.close()
            esM.close()
            return

        relExpV = [(Z,), (Z,)]
        relPSe = [Z, Z]
        relKT = [Z] * 4
        relPT = [Z, Z]
        relBsl = [Z, Z]
        for tt in range(NTT):
            eb = tt % 2
            p.wait("vector", *relExpV[eb])
            lastpub = Z
            sv = Z
            tv = Z
            for oc in range(8):
                pb = oc % 2
                bslb = oc % 2
                p.wait("sync", relBsl[bslb])
                bv = p.dma(
                    "sync",
                    A(kvbsl, bslb * 4 * 512, [2 * 4 * 512, P], [512, 4],
                      [1, 512]),
                    A(self.wkvbB, oc * 512, [H * (NOPE + VD), P],
                      [P * H * (NOPE + VD), 4], [1, 512]), "dS")
                p.wait("tensor", rkvT_rdy, bv, relPSe[pb])
                for c in range(4):
                    fn = (lambda e, c=c, tt=tt, oc=oc, pb=pb, bslb=bslb:
                          e.matmul(
                              self.PS[pb][:, 0:512],
                              rkvT[:, c * T + tt * P:c * T + (tt + 1) * P],
                              kvbsl[:, bslb * 4 * 512 + c * 512:
                                    bslb * 4 * 512 + (c + 1) * 512],
                              start=(c == 0), stop=(c == 3)))
                    if c == 3:
                        mv = p.inc("tensor", fn, "tS")
                    else:
                        p.run("tensor", fn)
                relBsl[bslb] = mv
                p.wait("vector", mv)
                relPSe[pb] = p.inc(
                    "vector", lambda e, pb=pb, eb=eb, oc=oc: e.tensor_copy(
                        kvexp[:, eb * 4096 + oc * 512:
                              eb * 4096 + (oc + 1) * 512],
                        self.PS[pb][:, 0:512]), "vS")
            kvexp_rdy = p.val("vS")
            for h in range(H):
                vsl = kvexp[:, eb * 4096 + h * 256 + NOPE:
                            eb * 4096 + (h + 1) * 256]
                p.wait("gpsimd", kvexp_rdy)
                lastpub = p.dma(
                    "gpsimd",
                    A(self.kvown, V_OFF + h * T * VD + tt * P * VD,
                      [VD, P], [1, VD]),
                    vsl, "dG")
                nsl = kvexp[:, eb * 4096 + h * 256:eb * 4096 + h * 256 + NOPE]
                sl = h % 2
                p.wait("tensor", kvexp_rdy, relPT[sl])
                tv = p.inc("tensor", lambda e, nsl=nsl, sl=sl: e.transpose(
                    self.PTx[sl][:, 0:P], nsl, self.iden_b[:]),
                    "tS")
                kb = h % 4
                p.wait("vector", tv, relKT[kb])
                relPT[sl] = p.inc("vector", lambda e, kb=kb, sl=sl:
                                  e.tensor_copy(
                                      kTst[:, kb * P:(kb + 1) * P],
                                      self.PTx[sl][:, 0:P]), "vS")
                p.wait("gpsimd", relPT[sl])
                relKT[kb] = p.dma(
                    "gpsimd",
                    A(self.kvown, KHI_OFF + h * P * T + tt * P, [T, P], [1, P]),
                    kTst[:, kb * P:(kb + 1) * P], "dG")
                p.wait("scalar", kvexp_rdy)
                sv = p.inc("scalar", lambda e, nsl=nsl, h=h, tt=tt:
                           e.activation(knsc[:, 0:NOPE], nsl, AF.Square,
                                        accum_out=knall[:, h * 8 + tt:
                                                        h * 8 + tt + 1]),
                           "sS")
            relExpV[eb] = (lastpub, sv, tv)
        for tt in range(NTT):
            sl = tt % 2
            p.wait("tensor", p.val("vS"), relPT[sl])
            tv = p.inc("tensor", lambda e, tt=tt, sl=sl: e.transpose(
                self.PTx[sl][0:ROPE, 0:P],
                krall[:, tt * ROPE:(tt + 1) * ROPE], self.iden_b[:]), "tS")
            kb = tt % 4
            p.wait("vector", tv, relKT[kb])
            relPT[sl] = p.inc("vector", lambda e, kb=kb, sl=sl: e.tensor_copy(
                kTst[0:ROPE, kb * P:(kb + 1) * P],
                self.PTx[sl][0:ROPE, 0:P]), "vS")
            p.wait("gpsimd", relPT[sl])
            relKT[kb] = p.dma(
                "gpsimd",
                A(self.kvown, KLO_OFF + tt * P, [T, ROPE], [1, P]),
                kTst[0:ROPE, kb * P:(kb + 1) * P], "dG")
        p.wait("vector", p.val("sS"))
        for tt in range(NTT):
            p.inc("vector", lambda e, tt=tt: e.tensor_single_scalar(
                A(knall, tt, [P, P], [8, H]), A(knall, tt, [P, P], [8, H]),
                krn[:, tt:tt + 1], ALU.add), "vS")
        p.wait("tensor", p.val("vS"))
        tv = p.inc("tensor", lambda e: e.transpose(
            self.PS[0][:, 0:128], knall[:], self.iden_f[:]), "tS")
        p.wait("vector", tv)
        v = p.inc("vector", lambda e: e.reduce_max(
            st[:, 0:1], self.PS[0][:, 0:128], axis=AX.X), "vS")
        p.wait("tensor", v)
        tv = p.inc("tensor", lambda e: e.transpose(
            self.PS[1][0:1, 0:128], st[:, 0:1], self.iden_f[:]), "tS")
        p.wait("vector", tv)
        v = p.inc("vector", lambda e: e.reduce_max(
            A(st, 32, [64, 1], [1, H]),
            A(self.PS[1], 0, [512, 1], [8, H], [1, NTT]), axis=AX.X), "vS")
        p.wait("vector", v)
        p.inc("vector", lambda e: e.memset(self.knrow_b[:], 0.0), "vS")
        v = p.inc("vector", lambda e: e.tensor_copy(
            self.knrow_b[0:1, 0:H], A(st, 32, [64, 1], [1, H])), "vS")
        p.wait("gpsimd", v)
        p.dma("gpsimd", A(self.kvown, KN_OFF, [P, 1], [1, P]),
              self.knrow_b[0:1, 0:P], "dG")
        if CUT == "pub":
            p.barrier()
            esK.close()
            esL.close()
            esM.close()
            return
        p.wait("gpsimd", p.val("dG"))
        ccv = p.inc("gpsimd", lambda e: e.collective_compute(
            "AllGather", ALU.bypass, replica_groups=[list(range(NC_))],
            ins=[self.kvown.ap()], outs=[self.kvall.ap()]), "cc")
        esK.close()
        if CUT == "ag":
            p.barrier()
            esL.close()
            esM.close()
            return

        esQ = ExitStack()
        RQ = lambda nm, shp, dt: esQ.enter_context(
            nc.sbuf_tensor(nm, shp, dt, side="right"))
        wqa_sb = RQ("wqa_sb", [P, 16 * QR], BF16)
        qcf = RQ("qcf", [P, 2 * QR], F32)
        qcs = RQ("qcs", [P, 2 * QR], BF16)
        p.wait("sync", p.val("dG"), p.val("tS"), p.val("vS"), p.val("sS"))
        wqa_rdy = p.dma(
            "sync",
            A(wqa_sb, 0, [16 * QR, P], [QR, 16], [1, QR]),
            A(self.wqaB, 0, [QR, P], [P * QR, 16], [1, QR]), "dS")
        relPSa = [Z, Z]
        relPSb = [Z, Z]
        relQcS = [Z, Z]
        relQcsT = [Z, Z]
        relPT = [Z, Z]
        for tt in range(NTT):
            sb = tt % 2
            pa, pb_ = sb, 2 + sb
            p.wait("tensor", ln2T_rdy, wqa_rdy, relPSa[sb], relPSb[sb])
            for c in range(16):
                fa = (lambda e, c=c, tt=tt, pa=pa: e.matmul(
                    self.PS[pa][:, 0:512],
                    ln2T[:, c * T + tt * P:c * T + (tt + 1) * P],
                    wqa_sb[:, c * QR:c * QR + 512],
                    start=(c == 0), stop=(c == 15)))
                fb = (lambda e, c=c, tt=tt, pb_=pb_: e.matmul(
                    self.PS[pb_][:, 0:256],
                    ln2T[:, c * T + tt * P:c * T + (tt + 1) * P],
                    wqa_sb[:, c * QR + 512:(c + 1) * QR],
                    start=(c == 0), stop=(c == 15)))
                if c == 15:
                    mva = p.inc("tensor", fa, "tS")
                    mvb = p.inc("tensor", fb, "tS")
                else:
                    p.run("tensor", fa)
                    p.run("tensor", fb)
            p.wait("vector", mva, relQcS[sb])
            relPSa[sb] = p.inc("vector", lambda e, pa=pa, sb=sb: e.tensor_copy(
                qcf[:, sb * QR:sb * QR + 512], self.PS[pa][:, 0:512]), "vS")
            p.wait("vector", mvb)
            cv = p.inc("vector", lambda e, pb_=pb_, sb=sb: e.tensor_copy(
                qcf[:, sb * QR + 512:(sb + 1) * QR], self.PS[pb_][:, 0:256]),
                "vS")
            relPSb[sb] = cv
            p.wait("scalar", cv)
            sv = p.inc("scalar", lambda e, sb=sb: e.activation(
                lnscr[:, 0:QR], qcf[:, sb * QR:(sb + 1) * QR], AF.Square,
                accum_out=st[:, 32:33]), "sS")
            p.wait("scalar", sv)
            sv = p.inc("scalar", lambda e: e.activation(
                st[:, 40:41], st[:, 32:33], AF.Sqrt,
                scale=np.float32(1.0 / QR), bias=np.float32(1e-6)), "sS")
            p.wait("vector", sv)
            v = p.inc("vector", lambda e: e.reciprocal(
                st[:, 48:49], st[:, 40:41]), "vS")
            p.wait("scalar", v, relQcsT[sb])
            rv = p.inc("scalar", lambda e, sb=sb: e.activation(
                qcs[:, sb * QR:(sb + 1) * QR],
                qcf[:, sb * QR:(sb + 1) * QR],
                AF.Identity, scale=st[:, 48:49]), "sS")
            relQcS[sb] = rv
            for c in range(6):
                sl = c % 2
                p.wait("tensor", rv, relPT[sl])
                tv = p.inc("tensor", lambda e, c=c, sl=sl, sb=sb: e.transpose(
                    self.PTx[sl][:, 0:P],
                    qcs[:, sb * QR + c * P:sb * QR + (c + 1) * P],
                    self.iden_b[:]), "tS")
                if c == 5:
                    relQcsT[sb] = tv
                p.wait("vector", tv)
                relPT[sl] = p.inc(
                    "vector", lambda e, c=c, tt=tt, sl=sl: e.tensor_copy(
                        qcsT[:, c * T + tt * P:c * T + (tt + 1) * P],
                        self.PTx[sl][:, 0:P]), "vS")
        qcsT_rdy = p.val("vS")
        esQ.close()
        esL.close()
        if CUT == "qc":
            p.barrier()
            esM.close()
            return

        esQ2 = ExitStack()
        RQ2 = lambda nm, shp, dt: esQ2.enter_context(
            nc.sbuf_tensor(nm, shp, dt, side="right"))
        wqb_sb = RQ2("wqb_sb", [P, 6 * H * QKD], BF16)
        stn = RQ2("stn", [P, P], BF16)
        sto = RQ2("sto", [P, P], BF16)
        qTst = RQ2("qTst", [P, 2 * P], BF16)
        qTst2 = RQ2("qTst2", [65, 2 * P], BF16)
        rop2 = RQ2("rop2", [P, 2 * ROPE], F32)
        p.wait("sync", qcsT_rdy, p.val("tS"))
        wqb_rdy = p.dma(
            "sync",
            A(wqb_sb, 0, [6 * H * QKD, P], [H * QKD, 6], [1, H * QKD]),
            A(self.wqbB, 0, [H * QKD, P], [P * H * QKD, 6], [1, H * QKD]),
            "dS")
        p.run("vector", lambda e: e.memset(sto[:, 65:P], 0.0))
        p.wait("gpsimd", ccv)
        gv = p.dma("gpsimd", A(lnscr, 0, [D, 1], [16, NC_], [1, H]),
                   A(self.kvall, KN_OFF, [SZKV, 1], [SZKV, NC_], [1, H]),
                   "dG")
        p.wait("vector", gv)
        v = p.inc("vector", lambda e: e.reduce_max(
            A(st, 32, [64, 1], [1, H]),
            A(lnscr, 0, [D, 1], [1, H], [16, NC_]), axis=AX.X), "vS")
        p.wait("vector", v)
        v = p.inc("vector", lambda e: e.tensor_copy(
            self.knrow_b[0:1, 0:H], A(st, 32, [64, 1], [1, H])), "vS")
        p.wait("tensor", v, p.val("vS"))
        tv = p.inc("tensor", lambda e: e.matmul(
            self.PS[2][:, 0:H], self.ones_r[:], self.knrow_b[0:1, 0:H],
            start=True, stop=True), "tS")
        p.wait("vector", tv)
        kmx = p.inc("vector", lambda e: e.tensor_copy(
            self.kmaxbc[:], self.PS[2][:, 0:H]), "vS")

        relPS = [Z, Z]
        relStn = Z
        relSto = Z
        relQT = [Z, Z]
        relQT2 = [Z, Z]
        relPT = [Z, Z]
        for h in range(H):
            for tt in range(NTT):
                pq = (h * NTT + tt) % 2
                p.wait("tensor", qcsT_rdy, wqb_rdy, relPS[pq])
                for c in range(6):
                    fn = (lambda e, c=c, tt=tt, h=h, pq=pq: e.matmul(
                        self.PS[pq][:, 0:QKD],
                        qcsT[:, c * T + tt * P:c * T + (tt + 1) * P],
                        wqb_sb[:, c * H * QKD + h * QKD:
                               c * H * QKD + (h + 1) * QKD],
                        start=(c == 0), stop=(c == 5)))
                    if c == 5:
                        mv = p.inc("tensor", fn, "tS")
                    else:
                        p.run("tensor", fn)
                p.wait("vector", mv, relStn, cosrdy)
                p.inc("vector", lambda e, pq=pq: e.tensor_copy(
                    stn[:], self.PS[pq][:, 0:NOPE]), "vS")
                co = cosb[:, tt * ROPE:(tt + 1) * ROPE]
                sn = snb[:, tt * ROPE:(tt + 1) * ROPE]
                rt, rs = rop2[:, 0:ROPE], rop2[:, ROPE:2 * ROPE]
                p.inc("vector", lambda e, pq=pq, co=co, rt=rt: e.tensor_tensor(
                    rt, self.PS[pq][:, NOPE:QKD], co, ALU.mult), "vS")
                p.inc("vector", lambda e, pq=pq, sn=sn, rs=rs: e.tensor_tensor(
                    rs[:, 0:32], self.PS[pq][:, NOPE + 32:QKD], sn[:, 0:32],
                    ALU.mult), "vS")
                v = p.inc("vector", lambda e, pq=pq, sn=sn, rs=rs:
                          e.tensor_tensor(rs[:, 32:64],
                                          self.PS[pq][:, NOPE:NOPE + 32],
                                          sn[:, 32:64], ALU.mult), "vS")
                p.wait("scalar", v, kmx)
                sv = p.inc("scalar", lambda e, pq=pq: e.activation(
                    lnscr[:, 0:QKD], self.PS[pq][:, 0:QKD], AF.Square,
                    accum_out=st[:, 33:34]), "sS")
                relPS[pq] = sv
                p.wait("scalar", sv)
                bv = p.inc("scalar", lambda e, h=h: e.activation(
                    st[:, 41:42], st[:, 33:34], AF.Sqrt,
                    scale=self.kmaxbc[:, h:h + 1]), "sS")
                p.wait("vector", v, bv, relSto)
                p.inc("vector", lambda e, rt=rt, rs=rs: e.tensor_tensor(
                    sto[:, 0:ROPE], rt, rs, ALU.add), "vS")
                v = p.inc("vector", lambda e: e.tensor_copy(
                    sto[:, ROPE:ROPE + 1], st[:, 41:42]), "vS")
                sl = 0
                p.wait("tensor", p.val("vS"), relPT[sl])
                tv = p.inc("tensor", lambda e, sl=sl: e.transpose(
                    self.PTx[sl][:, 0:P], stn[:], self.iden_b[:]),
                    "tS")
                relStn = tv
                qb = (h * NTT + tt) % 2
                p.wait("vector", tv, relQT[qb])
                relPT[sl] = p.inc("vector", lambda e, qb=qb, sl=sl:
                                  e.tensor_copy(qTst[:, qb * P:(qb + 1) * P],
                                                self.PTx[sl][:, 0:P]),
                                  "vS")
                p.wait("gpsimd", relPT[sl])
                relQT[qb] = p.dma(
                    "gpsimd",
                    A(self.qhi_dd, h * NOPE * T + tt * P, [T, P], [1, P]),
                    qTst[:, qb * P:(qb + 1) * P], "dG")
                sl2 = 1
                p.wait("tensor", relPT[sl2])
                tv = p.inc("tensor", lambda e, sl2=sl2: e.transpose(
                    self.PTx[sl2][:, 0:P], sto[:],
                    self.iden_b[:]), "tS")
                relSto = tv
                p.wait("vector", tv, relQT2[qb])
                relPT[sl2] = p.inc(
                    "vector", lambda e, qb=qb, sl2=sl2: e.tensor_copy(
                        qTst2[0:65, qb * P:(qb + 1) * P],
                        self.PTx[sl2][0:65, 0:P]), "vS")
                p.wait("gpsimd", relPT[sl2])
                relQT2[qb] = p.dma(
                    "gpsimd",
                    A(self.qlo_dd, h * 65 * T + tt * P, [T, 65], [1, P]),
                    qTst2[0:65, qb * P:(qb + 1) * P], "dG")
        p.barrier()
        esQ2.close()
        esM.close()

    def _attn_pv(self, prev, v_st, probs, relProbs):
        p = self.p
        ev, kb, kc, qt, pc_, first, dv = prev
        p.wait("tensor", ev, dv)
        p.run("tensor",
              lambda e, kb=kb, kc=kc, qt=qt, pc_=pc_,
              first=first: e.matmul(
                  self.PS[2 + qt][:, 0:512],
                  v_st[:, kb * T + kc * P:
                       kb * T + (kc + 1) * P],
                  probs[:, pc_ * 512:(pc_ + 1) * 512],
                  start=first, stop=False))
        relProbs[pc_] = p.inc(
            "tensor", lambda e, qt=qt, pc_=pc_, first=first:
            e.matmul(self.PR[qt][0:1, 0:512], self.ones_c[:],
                     probs[:, pc_ * 512:(pc_ + 1) * 512],
                     start=first, stop=False), "tS")

    def _diag_pv(self, prevd, v_own, pdm):
        p = self.p
        mk, hb, pd_, j, qt = prevd
        bq = qt * 4 + j
        last = (j == 3)
        p.wait("tensor", mk)
        p.run("tensor", lambda e, hb=hb, bq=bq, j=j, qt=qt,
              pd_=pd_, last=last: e.matmul(
                  self.PS[2 + qt][:, j * P:(j + 1) * P],
                  v_own[:, hb * T + bq * P:hb * T + (bq + 1) * P],
                  pdm[:, pd_ * P:(pd_ + 1) * P],
                  start=False, stop=last))
        return p.inc(
            "tensor", lambda e, j=j, qt=qt, pd_=pd_, last=last:
            e.matmul(self.PR[qt][0:1, j * P:(j + 1) * P],
                     self.ones_c[:],
                     pdm[:, pd_ * P:(pd_ + 1) * P],
                     start=False, stop=last), "tS")

    def attention_phase(self):
        nc, p = self.nc, self.p
        self.esA = ExitStack()
        self.attn_fm = self.esA.enter_context(
            nc.sbuf_tensor("attn_fm", [P, H * T], BF16, side="right"))
        esA2 = ExitStack()
        R2 = lambda nm, shp, dt: esA2.enter_context(
            nc.sbuf_tensor(nm, shp, dt, side="right"))
        khi_st = R2("khi_st", [P, 2 * T], BF16)
        v_st = R2("v_st", [P, 2 * T], BF16)
        khi_own = R2("khi_own", [P, 2 * T], BF16)
        v_own = R2("v_own", [P, 2 * T], BF16)
        qhi_st = R2("qhi_st", [P, 2 * T], BF16)
        qlo_st = R2("qlo_st", [73, 2 * T], BF16)
        klo_own = R2("klo_own", [65, T], BF16)
        qflag_b = R2("qflag_b", [NTT, T], BF16)
        fstg = R2("fstg", [NTT, T], F32)
        probs = R2("probs", [P, 3 * 512], BF16)
        probsd = R2("probsd", [P, 2 * P], BF16)
        pdm = R2("pdm", [P, 2 * P], BF16)
        rsf = R2("rsf", [1, 512], F32)
        rhib = R2("rhib", [1, 1024], BF16)
        bcsb = R2("bcsb", [P, 512], F32)

        p.run("vector", lambda e: e.memset(klo_own[64:65, :], -1.0))
        dvf = p.dma("sync", fstg[:], self.qflag_d[:, :], "dS")
        p.wait("vector", dvf)
        cv = p.inc("vector", lambda e: e.tensor_copy(qflag_b[:], fstg[:]),
                   "vS")
        lv = p.dma("gpsimd", klo_own[0:ROPE, :],
                   A(self.kvown, KLO_OFF, [T, ROPE], [1, T]), "dG")
        for r in range(NC_):
            lv = p.dma("gpsimd",
                       self.klo_all[0:ROPE, r * T:(r + 1) * T],
                       A(self.kvall, r * SZKV + KLO_OFF, [T, ROPE], [1, T]),
                       "dG")
        p.wait("gpsimd", cv, p.val("vS"))
        for b in range(2):
            lv = p.dma("gpsimd", qlo_st[65:73, b * T:(b + 1) * T],
                       qflag_b[:], "dG")

        relQ = [Z, Z]
        relOwn = [Z, Z]
        relKhi = [Z, Z]
        relV = [Z, Z]
        relProbs = [Z, Z, Z]
        relProbsD = [Z, Z]
        relPdm = [Z, Z]
        relSc = [Z, Z]
        relPV = [Z, Z]
        relSum = [Z, Z]
        relBc = Z
        dkh = [Z, Z]
        dvv = [Z, Z]
        pc = 0
        sc = 0
        for h in range(H):
            hb = h % 2
            p.wait("sync", relQ[hb])
            p.dma("sync", qhi_st[:, hb * T:(hb + 1) * T],
                  A(self.qhi_dd, h * NOPE * T, [T, P], [1, T]), "dS")
            dq2 = p.dma("sync", qlo_st[0:65, hb * T:(hb + 1) * T],
                        A(self.qlo_dd, h * 65 * T, [T, 65], [1, T]), "dS")
            p.wait("gpsimd", relOwn[hb])
            p.dma("gpsimd", khi_own[:, hb * T:(hb + 1) * T],
                  A(self.kvown, KHI_OFF + h * P * T, [T, P], [1, T]), "dG")
            dow2 = p.dma("gpsimd",
                         A(v_own, hb * T, [2 * T, P], [P, NTT], [1, P]),
                         A(self.kvown, V_OFF + h * T * VD,
                           [VD, P], [P * VD, NTT], [1, VD]), "dG")
            for r in range(NC_):
                kb = (h * NC_ + r) % 2
                p.wait("sync", relKhi[kb])
                dkh[kb] = p.dma(
                    "sync", khi_st[:, kb * T:(kb + 1) * T],
                    A(self.kvall, r * SZKV + KHI_OFF + h * P * T,
                      [T, P], [1, T]), "dS")
                p.wait("gpsimd", relV[kb])
                dvv[kb] = p.dma(
                    "gpsimd", A(v_st, kb * T, [2 * T, P], [P, NTT], [1, P]),
                    A(self.kvall, r * SZKV + V_OFF + h * T * VD,
                      [VD, P], [P * VD, NTT], [1, VD]), "dG")
                prev = None
                for kc in range(NTT):
                    for qt in range(2):
                        s_ = sc % 2
                        sc += 1
                        first = (r == 0 and kc == 0)
                        p.wait("tensor", dkh[kb], dq2, lv, relSc[s_])
                        if first:
                            p.wait("tensor", relPV[qt], relSum[qt])
                        p.run("tensor",
                              lambda e, kb=kb, kc=kc, qt=qt, s_=s_, hb=hb:
                              e.matmul(self.PS[s_][:, 0:512],
                                       khi_st[:, kb * T + kc * P:
                                              kb * T + (kc + 1) * P],
                                       qhi_st[:, hb * T + qt * 512:
                                              hb * T + (qt + 1) * 512],
                                       start=True, stop=False))
                        mv = p.inc("tensor",
                                   lambda e, r=r, kc=kc, qt=qt, s_=s_, hb=hb:
                                   e.matmul(
                                       self.PS[s_][:, 0:512],
                                       self.klo_all[0:73,
                                                    r * T + kc * P:
                                                    r * T + (kc + 1) * P],
                                       qlo_st[0:73,
                                              hb * T + qt * 512:
                                              hb * T + (qt + 1) * 512],
                                       start=False, stop=True), "tS")
                        pc_ = pc % 3
                        pc += 1
                        p.wait("scalar", mv, relProbs[pc_])
                        ev = p.inc("scalar", lambda e, s_=s_, pc_=pc_:
                                   e.activation(
                                       probs[:, pc_ * 512:(pc_ + 1) * 512],
                                       self.PS[s_][:, 0:512], AF.Exp), "sS")
                        relSc[s_] = ev
                        if prev is not None:
                            self._attn_pv(prev, v_st, probs, relProbs)
                        prev = (ev, kb, kc, qt, pc_, first, dvv[kb])
                self._attn_pv(prev, v_st, probs, relProbs)
                relKhi[kb] = p.val("tS")
                relV[kb] = p.val("tS")
            for qt in range(2):
                prevd = None
                for j in range(4):
                    bq = qt * 4 + j
                    s_ = sc % 2
                    sc += 1
                    p.wait("tensor", dow2, lv, relSc[s_])
                    p.run("tensor", lambda e, hb=hb, bq=bq, s_=s_: e.matmul(
                        self.PS[s_][:, 0:P],
                        khi_own[:, hb * T + bq * P:hb * T + (bq + 1) * P],
                        qhi_st[:, hb * T + bq * P:hb * T + (bq + 1) * P],
                        start=True, stop=False))
                    mv = p.inc("tensor", lambda e, hb=hb, bq=bq, s_=s_:
                               e.matmul(
                                   self.PS[s_][:, 0:P],
                                   klo_own[0:65, bq * P:(bq + 1) * P],
                                   qlo_st[0:65, hb * T + bq * P:
                                          hb * T + (bq + 1) * P],
                                   start=False, stop=True), "tS")
                    pd_ = j % 2
                    p.wait("scalar", mv, relProbsD[pd_])
                    ev = p.inc("scalar", lambda e, s_=s_, pd_=pd_:
                               e.activation(probsd[:, pd_ * P:(pd_ + 1) * P],
                                            self.PS[s_][:, 0:P], AF.Exp),
                               "sS")
                    relSc[s_] = ev
                    p.wait("vector", ev, relPdm[pd_])
                    mk = p.inc("vector", lambda e, pd_=pd_: e.tensor_tensor(
                        pdm[:, pd_ * P:(pd_ + 1) * P],
                        probsd[:, pd_ * P:(pd_ + 1) * P], self.trim_b[:],
                        ALU.mult), "vS")
                    relProbsD[pd_] = mk
                    if prevd is not None:
                        relPdm[prevd[2]] = self._diag_pv(prevd, v_own, pdm)
                    prevd = (mk, hb, pd_, j, qt)
                relPdm[prevd[2]] = self._diag_pv(prevd, v_own, pdm)
                p.wait("vector", p.val("tS"))
                v = p.inc("vector", lambda e, qt=qt: e.reciprocal(
                    rsf[:], self.PR[qt][0:1, 0:512]), "vS")
                relSum[qt] = v
                p.wait("vector", v)
                v = p.inc("vector", lambda e: e.tensor_copy(
                    rhib[0:1, 0:512], rsf[:]), "vS")
                p.wait("vector", v)
                v = p.inc("vector", lambda e: e.tensor_tensor(
                    rhib[0:1, 512:1024], rsf[:], rhib[0:1, 0:512],
                    ALU.subtract), "vS")
                bcb = sc % 2
                sc += 1
                p.wait("tensor", v, relSc[bcb])
                p.run("tensor", lambda e, bcb=bcb: e.matmul(
                    self.PS[bcb][:, 0:512], self.ones_r[:], rhib[0:1, 0:512],
                    start=True, stop=False))
                tv = p.inc("tensor", lambda e, bcb=bcb: e.matmul(
                    self.PS[bcb][:, 0:512], self.ones_r[:],
                    rhib[0:1, 512:1024], start=False, stop=True), "tS")
                p.wait("vector", tv)
                v = p.inc("vector", lambda e, bcb=bcb: e.tensor_copy(
                    bcsb[:], self.PS[bcb][:, 0:512]), "vS")
                relSc[bcb] = v
                p.wait("vector", v)
                relPV[qt] = p.inc(
                    "vector", lambda e, h=h, qt=qt: e.tensor_tensor(
                        self.attn_fm[:, h * T + qt * 512:
                                     h * T + (qt + 1) * 512],
                        self.PS[2 + qt][:, 0:512], bcsb[:], ALU.mult), "vS")
            relQ[hb] = p.val("tS")
            relOwn[hb] = p.val("tS")
        p.barrier()
        esA2.close()

    def wo_phase(self):
        nc, p = self.nc, self.p
        esW = ExitStack()
        woslab = esW.enter_context(
            nc.sbuf_tensor("woslab", [P, 2 * 16 * 512], BF16, side="right"))
        relSlab = [Z, Z]
        relPS = [Z, Z]
        for oc in range(4):
            wb = oc % 2
            p.wait("sync", relSlab[wb])
            dv = p.dma(
                "sync",
                A(woslab, wb * 16 * 512, [2 * 16 * 512, P], [512, 16],
                  [1, 512]),
                A(self.woB, oc * 512, [D, P], [P * D, 16], [1, 512]), "dS")
            for tt in range(NTT):
                pb = (oc * NTT + tt) % 2
                p.wait("tensor", dv, relPS[pb])
                for hh in range(16):
                    fn = (lambda e, hh=hh, tt=tt, oc=oc, pb=pb, wb=wb:
                          e.matmul(
                              self.PS[pb][:, 0:512],
                              self.attn_fm[:, hh * T + tt * P:
                                           hh * T + (tt + 1) * P],
                              woslab[:, wb * 16 * 512 + hh * 512:
                                     wb * 16 * 512 + (hh + 1) * 512],
                              start=(hh == 0), stop=(hh == 15)))
                    if hh == 15:
                        mv = p.inc("tensor", fn, "tS")
                    else:
                        p.run("tensor", fn)
                relSlab[wb] = mv
                xsl = self.xres[:, tt * D + oc * 512:tt * D + (oc + 1) * 512]
                p.wait("vector", mv)
                relPS[pb] = p.inc("vector", lambda e, x=xsl, pb=pb:
                                  e.tensor_tensor(x, x, self.PS[pb][:, 0:512],
                                                  ALU.add), "vS")
        p.barrier()
        esW.close()
        self.esA.close()

    def build(self):
        p = self.p
        self.phase0_and_wcast()
        self.phaseW()
        self.ffn_phase("F1")
        if self.stage >= 2:
            self.kv_q_phases()
        if self.stage >= 3:
            self.attention_phase()
            self.wo_phase()
        if self.stage >= 4:
            self.ffn_phase("F2")
        for tt in range(NTT):
            p.dma("sync", self.out_d[tt * P:(tt + 1) * P, :],
                  self.xres[:, tt * D:(tt + 1) * D], "dS")
        p.wait("sync", p.val("dS"))
        with self.nc.Block() as block:
            block.sync(lambda e: [fn(e) for fn in p.tapes["sync"]])
            block.gpsimd(lambda e: [fn(e) for fn in p.tapes["gpsimd"]])
            block.tensor(lambda e: [fn(e) for fn in p.tapes["tensor"]])
            block.vector(lambda e: [fn(e) for fn in p.tapes["vector"]])
            block.scalar(lambda e: [fn(e) for fn in p.tapes["scalar"]])
        self.es.close()
        return self.nc


def host_inputs(hs, wqa, gqa, wqb, wkva, gkva, wkvb, wo, wffn, bffn):
    hs = np.ascontiguousarray(np.asarray(hs, dtype=np.float32)).reshape(
        NC_ * T, D)
    e = np.arange(0, ROPE, 2, dtype=np.float32) / np.float32(ROPE)
    inv = (1.0 / (10000.0 ** e.astype(np.float64))).astype(np.float32)
    qflag = np.zeros((NTT, T), np.float32)
    for j in range(NTT):
        qflag[j, j * P:(j + 1) * P] = -30000.0
    trim = (np.arange(P)[:, None] <= np.arange(P)[None, :]).astype(np.float32)
    iden = np.eye(P, dtype=np.float32)
    gqa_c = np.ascontiguousarray(
        np.asarray(gqa, np.float32).reshape(QR // P, P).T)
    gkva_c = np.ascontiguousarray(
        np.asarray(gkva, np.float32).reshape(KVR // P, P).T)
    brep = np.ascontiguousarray(
        np.broadcast_to(np.asarray(bffn, np.float32)[None, :], (P, D)))
    com = dict(
        wqa=np.ascontiguousarray(np.asarray(wqa, np.float32)),
        gqa=gqa_c,
        wqb=np.ascontiguousarray(np.asarray(wqb, np.float32)),
        wkva=np.ascontiguousarray(np.asarray(wkva, np.float32)),
        gkva=gkva_c,
        wkvb=np.ascontiguousarray(np.asarray(wkvb, np.float32)),
        wo=np.ascontiguousarray(np.asarray(wo, np.float32)),
        wffn=np.ascontiguousarray(np.asarray(wffn, np.float32)),
        brep=brep, qflag=qflag, trim=trim, iden=iden)
    maps = []
    for c in range(NC_):
        pos = np.arange(c * T, (c + 1) * T, dtype=np.float32)
        ang = (pos[:, None] * inv[None, :]).astype(np.float32)
        ang = np.concatenate([ang, ang], axis=1)
        cosw = np.cos(ang.astype(np.float64)).astype(np.float32)
        snw = np.sin(ang.astype(np.float64)).astype(np.float32)
        snw[:, 0:32] = -snw[:, 0:32]
        kr = np.zeros((NTT, NC_ * T), np.float32)
        for j in range(NTT):
            for r in range(NC_):
                for kc in range(NTT):
                    if 8 * r + kc >= 8 * c + j:
                        kr[j, r * T + kc * P:r * T + (kc + 1) * P] = 1.0
        m = dict(com)
        m["x0"] = np.ascontiguousarray(hs[c * T:(c + 1) * T])
        m["cosw"] = cosw
        m["snw"] = np.ascontiguousarray(snw)
        m["krows"] = kr
        maps.append(m)
    return maps


def kernel(L0_attn_hidden_states, L0_attn_Wqa, L0_attn_g_qa, L0_attn_Wqb,
           L0_attn_Wkva, L0_attn_g_kva, L0_attn_Wkvb, L0_attn_Wo,
           L0_ffn_weight, L0_ffn_bias):
    maps = host_inputs(L0_attn_hidden_states, L0_attn_Wqa, L0_attn_g_qa,
                       L0_attn_Wqb, L0_attn_Wkva, L0_attn_g_kva,
                       L0_attn_Wkvb, L0_attn_Wo, L0_ffn_weight, L0_ffn_bias)
    nc = MK(STAGE).build()
    res = run_bass_kernel_spmd(nc, maps, core_ids=list(range(NC_)))
    out = np.concatenate([np.asarray(r["out"]) for r in res.results], axis=0)
    return np.ascontiguousarray(out).reshape(1, NC_ * T, D).astype(np.float32)
